# version 10
# baseline (speedup 1.0000x reference)
import numpy as np
import ml_dtypes
from contextlib import ExitStack

import concourse.bass as bass
import concourse.mybir as mybir
from concourse.bass_utils import run_bass_kernel_spmd

F32 = mybir.dt.float32
BF16 = mybir.dt.bfloat16
AF = mybir.ActivationFunctionType
ALU = mybir.AluOpType
AX = mybir.AxisListType
BFNP = ml_dtypes.bfloat16

S = 2048
D = 2048
SPC = 2
NCORES = 8
NCH = D // 128

DKV = 512
UP = 1024
NH = 16
RD = 64
SCALE0 = float(1.0 / np.sqrt(192.0))
I0 = 8192
HD = 128
SCALE1 = float(1.0 / np.sqrt(128.0))

ENGS = ("tensor", "vector", "scalar", "gpsimd", "sync")
SB_BASE = 16384 + 2560
PBYTES = 228352
PAIRSWAP = [i ^ 1 for i in range(32)]

DBG_STAGE = None


class MK:
    def __init__(self, wshapes):
        self.nc = bass.Bass("TRN2", target_bir_lowering=False, detect_race_conditions=False)
        self.ctx = ExitStack()
        self.ops = {e: [] for e in ENGS}
        self.sems = {}
        self.cnt = {}
        self.waited = {e: {} for e in ENGS}
        self.free = [(SB_BASE, PBYTES)]
        self.peak = 0
        self._nm = 0
        self.wshapes = wshapes
        self.bankfree = {}
        self._spans = {}
        self._tpi = 0

    def sem(self, name):
        if name not in self.sems:
            self.sems[name] = self.ctx.enter_context(self.nc.semaphore(name))
            self.cnt[name] = 0
        return self.sems[name]

    def val(self, name):
        self.sem(name)
        return self.cnt[name]

    def inc_only(self, name, amt=1):
        self.sem(name)
        self.cnt[name] += amt
        return self.cnt[name]

    def op(self, eng, meth, *a, incs=(), **kw):
        hs = [(self.sem(n), amt) for (n, amt) in incs]
        out = {}
        for (n, amt) in incs:
            self.cnt[n] += amt
            out[n] = self.cnt[n]
        def fn(e, meth=meth, a=a, kw=kw, hs=hs):
            inst = getattr(e, meth)(*a, **kw)
            for (h, amt) in hs:
                inst = inst.then_inc(h, amt)
        self.ops[eng].append(fn)
        return out

    def wait(self, eng, name, v):
        if v is None or v <= 0:
            return
        if self.waited[eng].get(name, 0) >= v:
            return
        self.waited[eng][name] = v
        h = self.sem(name)
        self.ops[eng].append(lambda e, h=h, v=v: e.wait_ge(h, v))

    def wait_tok(self, eng, tok):
        if tok is None:
            return
        if isinstance(tok, list):
            for t in tok:
                self.wait(eng, t[0], t[1])
            return
        self.wait(eng, tok[0], tok[1])

    def selfsync(self, eng):
        n = "self_" + eng
        r = self.op(eng, "nop", incs=[(n, 1)])
        self.wait(eng, n, r[n])

    def barrier(self):
        n = "bar"
        self.sem(n)
        base = self.cnt[n]
        self.op("vector", "engine_nop", incs=[(n, 1)])
        self.op("gpsimd", "engine_nop", incs=[(n, 1)])
        self.op("scalar", "activation", self.dummy[0:1, 0:1], self.dummy[0:1, 1:2],
                AF.Identity, incs=[(n, 1)])
        self.op("tensor", "nop", incs=[(n, 1)])
        self.op("sync", "nop", incs=[(n, 1)])
        for e in ENGS:
            self.wait(e, n, base + len(ENGS))

    def mm_chain(self, out_ap, pairs, incs=()):
        n = len(pairs)
        res = None
        for i, (l, r) in enumerate(pairs):
            res = self.op("tensor", "matmul", out_ap, l, r,
                          start=(i == 0), stop=(i == n - 1),
                          incs=incs if i == n - 1 else ())
        return res

    def bank(self, i):
        self.wait_tok("tensor", self.bankfree.get(i))
        return self.P[i]

    def free_bank(self, i, sem, v):
        self.bankfree[i] = (sem, v)

    def alloc(self, name, shape, dtype, align=64):
        es = mybir.dt.size(dtype)
        nb = int(np.prod(shape[1:])) * es if len(shape) > 1 else es
        nb = (nb + align - 1) // align * align
        for i, (a, b) in enumerate(self.free):
            st = (a + align - 1) // align * align
            if st + nb <= b:
                pieces = []
                if st > a:
                    pieces.append((a, st))
                if st + nb < b:
                    pieces.append((st + nb, b))
                self.free[i:i + 1] = pieces
                t = self.nc.alloc_sbuf_tensor_at(f"{name}_{self._nm}", list(shape), dtype, offset=st)
                self._nm += 1
                self._spans[id(t)] = (st, st + nb)
                self.peak = max(self.peak, st + nb)
                return t
        raise MemoryError(f"SBUF alloc failed: {name} {shape} {dtype} need {nb}B free={self.free}")

    def freebuf(self, *ts):
        for t in ts:
            self.free.append(self._spans.pop(id(t)))
        self.free.sort()
        merged = []
        for a, b in self.free:
            if merged and merged[-1][1] >= a:
                merged[-1][1] = max(merged[-1][1], b)
            else:
                merged.append([a, b])
        self.free = [tuple(x) for x in merged]

    def finalize(self):
        nc = self.nc
        with nc.Block() as block:
            for eng in ENGS:
                lst = self.ops[eng]
                getattr(block, eng)(lambda e, lst=lst: [fn(e) for fn in lst])
        self.ctx.close()
        return nc

    def loadw(self, dst, src_ap, semn="g_w"):
        return self.op("gpsimd", "dma_start", out=dst, in_=src_ap, incs=[(semn, 16)])[semn]

    def loadw_chunks(self, dst, w_ap, kchunks, cols, semn="g_w"):
        v = 0
        for c in range(kchunks):
            v = self.loadw(dst[:, c * cols:(c + 1) * cols], w_ap[c * 128:(c + 1) * 128, :], semn)
        return v

    def load_sync(self, dst, src_ap, semn="s_misc"):
        return self.op("sync", "dma_start", out=dst, in_=src_ap, incs=[(semn, 16)])[semn]

    def store_sync(self, dst_ap, src, semn="s_out"):
        return self.op("sync", "dma_start", out=dst_ap, in_=src, incs=[(semn, 16)])[semn]

    def alloc_ln(self, tag):
        return dict(
            st=self.alloc(f"lnst_{tag}", [128, 16], F32),
            sqd=self.alloc(f"lnsq_{tag}", [128, 2048], BF16),
            xln=self.alloc(f"lnx_{tag}", [128, 2 * 2048], BF16),
            par=[0],
            guard={},
        )

    def free_ln(self, ln):
        self.freebuf(ln["st"], ln["sqd"], ln["xln"])

    def emit_ln_tile(self, ln, xt, load_tok, xlnT, colbase, gt, evac_guards=()):
        st = ln["st"]
        self.wait_tok("vector", load_tok)
        self.wait_tok("scalar", load_tok)
        rs = self.op("vector", "reduce_sum", st[:, 0:1], xt[:, :], axis=AX.X,
                     incs=[("v_ln", 1)])
        rq = self.op("scalar", "activation", ln["sqd"][:, :], xt[:, :], AF.Square,
                     accum_out=st[:, 1:2], incs=[("s_ln", 1)])
        self.wait("vector", "v_ln", rs["v_ln"])
        self.wait("vector", "s_ln", rq["s_ln"])
        inv_d = 1.0 / float(D)
        def vs(meth, *a, **kw):
            r = self.op("vector", meth, *a, incs=[("v_ln", 1)], **kw)
            self.wait("vector", "v_ln", r["v_ln"])
            return r
        vs("tensor_scalar", st[:, 2:3], st[:, 0:1], inv_d, 0.0, ALU.mult, ALU.add)
        vs("tensor_mul", st[:, 3:4], st[:, 2:3], st[:, 2:3])
        vs("scalar_tensor_tensor", st[:, 4:5], st[:, 1:2], inv_d,
           st[:, 3:4], ALU.mult, ALU.subtract)
        vs("tensor_scalar", st[:, 5:6], st[:, 4:5], 1e-5, 0.0, ALU.add, ALU.add)
        r1 = vs("reciprocal", st[:, 6:7], st[:, 5:6])
        self.wait("scalar", "v_ln", r1["v_ln"])
        r2 = self.op("scalar", "activation", st[:, 7:8], st[:, 6:7], AF.Sqrt,
                     incs=[("s_ln", 1)])
        self.wait("vector", "s_ln", r2["s_ln"])
        vs("tensor_mul", st[:, 8:9], st[:, 2:3], st[:, 7:8])
        r3 = vs("tensor_scalar", st[:, 9:10], st[:, 8:9], -1.0, 0.0,
                ALU.mult, ALU.add)
        self.wait("scalar", "v_ln", r3["v_ln"])
        par = ln["par"][0]
        ln["par"][0] = 1 - par
        xln = ln["xln"][:, par * 2048:(par + 1) * 2048]
        self.wait_tok("scalar", ln["guard"].get(par))
        ap = self.op("scalar", "activation", xln, xt[:, :], AF.Identity,
                     bias=st[:, 9:10], scale=st[:, 7:8], incs=[("s_ln", 1)])

        def finish(ap=ap, xln=xln, par=par, xlnT=xlnT, colbase=colbase, gt=gt,
                   evac_guards=tuple(evac_guards)):
            self.wait("tensor", "s_ln", ap["s_ln"])
            tpv = None
            ltp = None
            for h in range(2):
                self._tpi = (self._tpi + 1) % 2
                bi = self._tpi
                bank = self.pt[bi]
                self.wait_tok("tensor", self._tpev.get(bi))
                tp = None
                for c in range(8 * h, 8 * h + 8):
                    tp = self.op("tensor", "transpose",
                                 bank[:, (c % 8) * 128:((c % 8) + 1) * 128],
                                 xln[:, c * 128:(c + 1) * 128], self.ident_bf[:, :],
                                 incs=[("t_tp", 1)])
                ltp = tp["t_tp"]
                self.wait("vector", "t_tp", ltp)
                for gtok in evac_guards:
                    self.wait_tok("vector", gtok)
                dst = xlnT[:, :].rearrange("p (c t) -> p c t", t=gt)[
                    :, 8 * h:8 * h + 8, colbase:colbase + 128]
                ev = self.op("vector", "tensor_copy", dst,
                             bank[:, :].rearrange("p (c t) -> p c t", t=128),
                             incs=[("v_tp", 1)])
                self._tpev[bi] = ("v_tp", ev["v_tp"])
                tpv = ev["v_tp"]
            ln["guard"][par] = ("t_tp", ltp)
            return ("v_tp", tpv)
        return ("s_ln", ap["s_ln"]), finish

    def emit_ln_pair(self, ln, tiles, xlnT, gt, evac_guards=()):
        st = ln["st"]
        n = len(tiles)
        rqs = []
        for i, (xt, lt, cb) in enumerate(tiles):
            self.wait_tok("vector", lt)
            self.wait_tok("scalar", lt)
            rs = self.op("vector", "reduce_sum", st[:, 0 + i:1 + i], xt[:, :], axis=AX.X,
                         incs=[("v_ln", 1)])
            rq = self.op("scalar", "activation", ln["sqd"][:, :], xt[:, :], AF.Square,
                         accum_out=st[:, 2 + i:3 + i], incs=[("s_ln", 1)])
            rqs.append(rq["s_ln"])
        self.wait("vector", "v_ln", rs["v_ln"])
        self.wait("vector", "s_ln", rq["s_ln"])
        inv_d = 1.0 / float(D)
        def vs(meth, *a, **kw):
            r = self.op("vector", meth, *a, incs=[("v_ln", 1)], **kw)
            self.wait("vector", "v_ln", r["v_ln"])
            return r
        vs("tensor_scalar", st[:, 4:4 + n], st[:, 0:n], inv_d, 0.0, ALU.mult, ALU.add)
        vs("tensor_mul", st[:, 6:6 + n], st[:, 4:4 + n], st[:, 4:4 + n])
        vs("scalar_tensor_tensor", st[:, 8:8 + n], st[:, 2:2 + n], inv_d,
           st[:, 6:6 + n], ALU.mult, ALU.subtract)
        vs("tensor_scalar", st[:, 10:10 + n], st[:, 8:8 + n], 1e-5, 0.0, ALU.add, ALU.add)
        r1 = vs("reciprocal", st[:, 12:12 + n], st[:, 10:10 + n])
        self.wait("scalar", "v_ln", r1["v_ln"])
        r2 = self.op("scalar", "activation", st[:, 14:14 + n], st[:, 12:12 + n], AF.Sqrt,
                     incs=[("s_ln", 1)])
        self.wait("vector", "s_ln", r2["s_ln"])
        vs("tensor_mul", st[:, 6:6 + n], st[:, 4:4 + n], st[:, 14:14 + n])
        r3 = vs("tensor_scalar", st[:, 4:4 + n], st[:, 6:6 + n], -1.0, 0.0,
                ALU.mult, ALU.add)
        self.wait("scalar", "v_ln", r3["v_ln"])
        if ln.get("nslot", 2) == 4:
            par = ln["par"][0]
            ln["par"][0] = 1 - par
        else:
            par = 0
        xbase = par * 2 * 2048
        self.wait_tok("scalar", ln["guard"].get(par))
        avs = []
        for i, (xt, lt, cb) in enumerate(tiles):
            ap = self.op("scalar", "activation",
                         ln["xln"][:, xbase + i * 2048:xbase + (i + 1) * 2048],
                         xt[:, :], AF.Identity, bias=st[:, 4 + i:5 + i],
                         scale=st[:, 14 + i:15 + i], incs=[("s_ln", 1)])
            avs.append(("s_ln", ap["s_ln"]))
        cons = [[("s_ln", avs[i][1])] for i in range(n)]

        def finish(tiles=tiles, avs=avs, xlnT=xlnT, gt=gt, par=par, xbase=xbase,
                   evac_guards=tuple(evac_guards)):
            ltp = None
            tpv = None
            for i, (xt, lt, cb) in enumerate(tiles):
                self.wait("tensor", avs[i][0], avs[i][1])
                for h in range(2):
                    self._tpi = (self._tpi + 1) % 2
                    bi = self._tpi
                    bank = self.pt[bi]
                    self.wait_tok("tensor", self._tpev.get(bi))
                    for c in range(8 * h, 8 * h + 8):
                        tp = self.op("tensor", "transpose",
                                     bank[:, (c % 8) * 128:((c % 8) + 1) * 128],
                                     ln["xln"][:, xbase + i * 2048 + c * 128:xbase + i * 2048 + (c + 1) * 128],
                                     self.ident_bf[:, :], incs=[("t_tp", 1)])
                    ltp = tp["t_tp"]
                    self.wait("vector", "t_tp", ltp)
                    for gtok in evac_guards:
                        self.wait_tok("vector", gtok)
                    dst = xlnT[:, :].rearrange("p (c t) -> p c t", t=gt)[
                        :, 8 * h:8 * h + 8, cb:cb + 128]
                    ev = self.op("vector", "tensor_copy", dst,
                                 bank[:, :].rearrange("p (c t) -> p c t", t=128),
                                 incs=[("v_tp", 1)])
                    self._tpev[bi] = ("v_tp", ev["v_tp"])
                    tpv = ev["v_tp"]
            ln["guard"][par] = ("t_tp", ltp)
            return ("v_tp", tpv)
        return cons, finish

    def setup(self):
        nc = self.nc
        self.x_in = nc.dram_tensor("x_in", [SPC, S, D], F32, kind="ExternalInput").ap()
        self.out = nc.dram_tensor("out", [SPC, S, D], F32, kind="ExternalOutput").ap()
        self.x_a = nc.dram_tensor("x_a", [SPC, S, D], F32, kind="Internal").ap()
        self.x_b = nc.dram_tensor("x_b", [SPC, S, D], F32, kind="Internal").ap()
        self.lncache = nc.dram_tensor("lncache", [SPC, NCH, 128, S], BF16, kind="Internal").ap()
        self.aocache = nc.dram_tensor("aocache", [SPC, NCH, 128, S], BF16, kind="Internal").ap()
        self.win = {}
        for name, shp in self.wshapes.items():
            self.win[name] = nc.dram_tensor(name, list(shp), F32, kind="ExternalInput").ap()

        theta = 1.0 / (10000.0 ** (np.arange(0, RD, 2, dtype=np.float64) / RD))
        ang = np.arange(S, dtype=np.float64)[:, None] * theta[None, :]
        A64 = np.repeat(np.sin(ang), 2, axis=1).T
        B64 = np.repeat(np.cos(ang), 2, axis=1).T
        sign = np.where(np.arange(RD) % 2 == 0, -1.0, 1.0)[:, None]
        ropeA_np = np.concatenate([A64, A64], 0).astype(np.float32)
        ropeBs_np = np.concatenate([B64 * sign, B64 * sign], 0).astype(np.float32)
        km = np.arange(128)[:, None]
        qm = np.arange(512)[None, :]
        masks = np.concatenate(
            [np.where(km + 128 * j <= qm, 0.0, -1e9).astype(np.float32) for j in range(4)], 1)
        masks01 = np.concatenate(
            [(km + 128 * j <= qm).astype(BFNP) for j in range(4)], 1)
        self.t_ident = nc.inline_tensor(np.eye(128, dtype=BFNP), name="identb").ap()
        self.t_identf = nc.inline_tensor(np.eye(128, dtype=np.float32), name="identf").ap()
        self.t_ropeA = nc.inline_tensor(ropeA_np, name="ropeA").ap()
        self.t_ropeBs = nc.inline_tensor(ropeBs_np, name="ropeBs").ap()
        self.t_masks = nc.inline_tensor(masks, name="diagmask").ap()
        self.t_masks01 = nc.inline_tensor(masks01, name="diagmask01").ap()

        self.dummy = self.alloc("dummy", [1, 2], F32)
        self.op("scalar", "memzero", self.dummy[:, :])
        self.ident_bf = self.alloc("identb", [128, 128], BF16)
        self.ident_f = self.alloc("identf", [128, 128], F32)
        self.ones_bf = self.alloc("ones_bf", [1, 128], BF16)
        self.ones_f = self.alloc("ones_f", [1, 128], F32)
        self.onescol_bf = self.alloc("onescol_bf", [128, 1], BF16)
        self.P = [self.ctx.enter_context(self.nc.psum_tensor(f"p{i}", [128, 512], F32))
                  for i in range(6)]
        self.pt = [self.ctx.enter_context(self.nc.psum_tensor(f"pt{i}", [128, 1024], BF16))
                   for i in range(2)]
        self._tpev = {}

        gv = self.loadw(self.ident_bf[:, :], self.t_ident[:, :])
        sv = self.op("sync", "dma_start", out=self.ident_f[:, :], in_=self.t_identf[:, :],
                     incs=[("s_misc", 16)])["s_misc"]
        self.op("vector", "memset", self.ones_bf[:, :], 1.0)
        self.op("vector", "memset", self.ones_f[:, :], 1.0)
        self.op("vector", "memset", self.onescol_bf[:, :], 1.0)
        for e in ENGS:
            self.wait(e, "g_w", gv)
            self.wait(e, "s_misc", sv)
        self.barrier()

    def l0a(self, x_src, x_dst):
        KcT = self.alloc("KcT", [128, 8 * S], BF16)
        krT = self.alloc("krT", [128, S], BF16)
        Vaug = self.alloc("Vaug", [128, 16 * 1040], BF16)
        for s in range(SPC):
            self.l0a_passA(s, x_src, KcT, krT, Vaug)
            self.barrier()
            self.l0a_passB(s, x_src, x_dst, KcT, krT, Vaug)
            self.barrier()
        self.freebuf(KcT, krT, Vaug)

    def l0a_passA(self, s, x_src, KcT, krT, Vaug):
        GT = 512
        w = self.win
        w_dkv = self.alloc("w_dkv", [128, NCH * DKV], BF16)
        w_upk = self.alloc("w_upk", [128, 4 * UP], BF16)
        w_upv = self.alloc("w_upv", [128, 4 * UP], BF16)
        w_kr = self.alloc("w_kr", [128, NCH * RD], BF16)
        b_dkv = self.alloc("b_dkv", [128, 4], F32)
        b_upk = self.alloc("b_upk", [128, 8], F32)
        b_upv = self.alloc("b_upv", [128, 8], F32)
        b_kr = self.alloc("b_kr", [64, 1], F32)
        ropeAk = self.alloc("ropeAk", [64, GT], F32)
        ropeBk = self.alloc("ropeBk", [64, GT], F32)
        xt = [self.alloc(f"xt{i}", [128, D], F32) for i in range(2)]
        xlnT2 = [self.alloc(f"xlnT{i}", [128, NCH * GT], BF16) for i in range(2)]
        ckvT = self.alloc("ckvT", [128, 4 * GT], BF16)
        vtmp = self.alloc("vtmp", [128, GT], BF16)
        krf = self.alloc("krf", [64, GT], F32)
        krf2 = self.alloc("krf2", [64, GT], F32)
        ln = dict(
            st=self.alloc("lnst_l0a", [128, 16], F32),
            sqd=self.alloc("lnsq_l0a", [128, 2048], BF16),
            xln=self.alloc("lnx_l0a", [128, 4 * 2048], BF16),
            par=[0], nslot=4, guard={},
        )

        wv_dkv = self.loadw_chunks(w_dkv, w["w_dkvT"], NCH, DKV, "w_dkv")
        wv_upk = self.loadw_chunks(w_upk, w["w_upkT"], 4, UP, "w_upk")
        wv_upv = self.loadw_chunks(w_upv, w["w_upvT"], 4, UP, "w_upv")
        wv_kr = self.loadw_chunks(w_kr, w["w_krT"], NCH, RD, "w_kr")
        self.load_sync(b_dkv[:, :], w["b_dkv_c"][:, :], "s_bias")
        self.load_sync(b_upk[:, :], w["b_upk_c"][:, :], "s_bias")
        self.load_sync(b_upv[:, :], w["b_upv_c"][:, :], "s_bias")
        bias_v = self.load_sync(b_kr[:, :], w["b_kr_c"][:, :], "s_bias")

        if True:
            self.op("vector", "memset",
                    Vaug[:, :].rearrange("p (b h e) -> p b h e", h=16, e=65)[:, :, :, 64:65],
                    1.0, incs=[("v_vau", 1)])
            ones_done = self.val("v_vau")

            xt_free = {0: None, 1: None}
            ckv_cons = None
            xlnT_cons = {0: None, 1: None}
            cache_guard = {0: None, 1: None}
            vtmp_free = None
            kr_cons = None

            def emit_group_ln(g):
                t0 = g * GT
                xlnT = xlnT2[g % 2]
                fins = []
                for pi in range(2):
                    tl = []
                    for b in range(2):
                        i = pi * 2 + b
                        self.wait_tok("sync", xt_free[b])
                        lv = self.load_sync(xt[b][:, :],
                                            x_src[s, t0 + i * 128:t0 + (i + 1) * 128, :],
                                            f"s_x{b}")
                        tl.append((xt[b], (f"s_x{b}", lv), i * 128))
                    guards = [t for t in (xlnT_cons[g % 2], cache_guard[g % 2])
                              if t is not None]
                    cons, fin = self.emit_ln_pair(ln, tl, xlnT, GT, evac_guards=guards)
                    for b in range(2):
                        xt_free[b] = cons[b]
                    fins.append(fin)
                return fins

            def flush_group_ln(g, fins):
                t0 = g * GT
                xlnT = xlnT2[g % 2]
                for fin in fins:
                    fin()
                tp_done = self.val("v_tp")
                self.wait("sync", "v_tp", tp_done)
                cs = 0
                for c in range(NCH):
                    cs = self.store_sync(self.lncache[s, c, :, t0:t0 + GT],
                                         xlnT[:, c * GT:(c + 1) * GT], "s_cache")
                cache_guard[g % 2] = ("s_cache", cs)
                return tp_done

            tp_done = flush_group_ln(0, emit_group_ln(0))
            nxt = emit_group_ln(1)
            for g in range(4):
                t0 = g * GT
                xlnT = xlnT2[g % 2]
                self.wait_tok("sync", kr_cons)
                self.load_sync(ropeAk[:, :], self.t_ropeA[0:64, t0:t0 + GT], "s_rope")
                ropek_v = self.load_sync(ropeBk[:, :], self.t_ropeBs[0:64, t0:t0 + GT], "s_rope")

                self.wait("tensor", "w_dkv", wv_dkv)
                self.wait("tensor", "v_tp", tp_done)
                for m in range(4):
                    pb = self.bank(m % 2)
                    self.wait_tok("scalar", ckv_cons)
                    pv = self.mm_chain(pb[:, :],
                        [(w_dkv[:, c * DKV + m * 128:c * DKV + (m + 1) * 128],
                          xlnT[:, c * GT:(c + 1) * GT]) for c in range(NCH)],
                        incs=[("t_ckv", 1)])["t_ckv"]
                    self.wait("scalar", "t_ckv", pv)
                    self.wait("scalar", "s_bias", bias_v)
                    r = self.op("scalar", "activation", ckvT[:, m * GT:(m + 1) * GT],
                                pb[:, :], AF.Identity, bias=b_dkv[:, m:m + 1],
                                incs=[("s_ckv", 1)])
                    self.free_bank(m % 2, "s_ckv", r["s_ckv"])
                ckv_done = self.val("s_ckv")
                self.wait("tensor", "s_ckv", ckv_done)

                self.wait("tensor", "w_upk", wv_upk)
                for mk in range(8):
                    pb = self.bank(2 + mk % 2)
                    pv = self.mm_chain(pb[:, :],
                        [(w_upk[:, c * UP + mk * 128:c * UP + (mk + 1) * 128],
                          ckvT[:, c * GT:(c + 1) * GT]) for c in range(4)],
                        incs=[("t_kc", 1)])["t_kc"]
                    self.wait("scalar", "t_kc", pv)
                    r = self.op("scalar", "activation", KcT[:, mk * S + t0:mk * S + t0 + GT],
                                pb[:, :], AF.Identity, bias=b_upk[:, mk:mk + 1],
                                incs=[("s_kc", 1)])
                    self.free_bank(2 + mk % 2, "s_kc", r["s_kc"])
                self.wait("tensor", "w_upv", wv_upv)
                for mk in range(8):
                    pb = self.bank(2 + mk % 2)
                    pv = self.mm_chain(pb[:, :],
                        [(w_upv[:, c * UP + mk * 128:c * UP + (mk + 1) * 128],
                          ckvT[:, c * GT:(c + 1) * GT]) for c in range(4)],
                        incs=[("t_kc", 1)])["t_kc"]
                    self.wait("scalar", "t_kc", pv)
                    self.wait_tok("scalar", vtmp_free)
                    r = self.op("scalar", "activation", vtmp[:, :], pb[:, :],
                                AF.Identity, bias=b_upv[:, mk:mk + 1], incs=[("s_kc", 1)])
                    self.free_bank(2 + mk % 2, "s_kc", r["s_kc"])
                    self.wait("tensor", "s_kc", r["s_kc"])
                    last_tp = None
                    for sub in range(4):
                        self._tpi = (self._tpi + 1) % 2
                        bi = self._tpi
                        bank = self.pt[bi]
                        self.wait_tok("tensor", self._tpev.get(bi))
                        pv2 = self.op("tensor", "transpose", bank[:, 0:128],
                                      vtmp[:, sub * 128:(sub + 1) * 128], self.ident_bf[:, :],
                                      incs=[("t_tp", 1)])["t_tp"]
                        last_tp = pv2
                        self.wait("vector", "t_tp", pv2)
                        self.wait("vector", "v_vau", ones_done)
                        blk = g * 4 + sub
                        dst = Vaug[:, blk * 1040 + 130 * mk:blk * 1040 + 130 * (mk + 1)]
                        dst = dst.rearrange("p (h e) -> p h e", h=2, e=65)[:, :, 0:64]
                        ev = self.op("vector", "tensor_copy", dst,
                                     bank[:, 0:128].rearrange("p (h e) -> p h e", h=2, e=64),
                                     incs=[("v_tp", 1)])
                        self._tpev[bi] = ("v_tp", ev["v_tp"])
                    vtmp_free = ("t_tp", last_tp)

                self.wait("tensor", "w_kr", wv_kr)
                pb = self.bank(4)
                self.wait_tok("tensor", kr_cons)
                pv = self.mm_chain(pb[0:64, :],
                    [(w_kr[:, c * RD:(c + 1) * RD],
                      xlnT[:, c * GT:(c + 1) * GT]) for c in range(NCH)],
                    incs=[("t_kr", 1)])["t_kr"]
                self.wait("scalar", "t_kr", pv)
                r = self.op("scalar", "activation", krf[:, :], pb[0:64, :],
                            AF.Identity, bias=b_kr[:, 0:1], incs=[("s_kr", 1)])
                self.free_bank(4, "s_kr", r["s_kr"])
                self.wait("vector", "s_kr", r["s_kr"])
                self.op("vector", "stream_shuffle", krf2[:, :], krf[:, :], PAIRSWAP)
                self.wait("vector", "s_rope", ropek_v)
                self.op("vector", "tensor_mul", krf[:, :], krf[:, :], ropeAk[:, :])
                self.op("vector", "tensor_mul", krf2[:, :], krf2[:, :], ropeBk[:, :])
                self.op("vector", "tensor_add", krT[0:64, t0:t0 + GT], krf[:, :], krf2[:, :])
                rk = self.op("vector", "tensor_copy", krT[64:128, t0:t0 + GT],
                             krT[0:64, t0:t0 + GT], incs=[("v_kr", 1)])
                kr_cons = ("v_kr", rk["v_kr"])

                xlnT_cons[g % 2] = ("t_kr", self.val("t_kr"))
                ckv_cons = ("t_kc", self.val("t_kc"))
                if g < 3:
                    tp_done = flush_group_ln(g + 1, nxt)
                if g < 2:
                    nxt = emit_group_ln(g + 2)

        self.freebuf(w_dkv, w_upk, w_upv, w_kr, b_dkv, b_upk, b_upv, b_kr,
                     ropeAk, ropeBk, xt[0], xt[1], xlnT2[0], xlnT2[1], ckvT,
                     vtmp, krf, krf2, ln["st"], ln["sqd"], ln["xln"])

    def l0a_passB(self, s, x_src, x_dst, KcT, krT, Vaug):
        GT = 512
        w = self.win
        w_dq = self.alloc("w_dq", [128, NCH * DKV], BF16)
        w_upq = self.alloc("w_upq", [128, 4 * UP], BF16)
        w_qr = self.alloc("w_qr", [128, 4 * UP], BF16)
        w_fc = self.alloc("w_fc", [128, 8 * D], BF16)
        b_dq = self.alloc("b_dq", [128, 4], F32)
        b_upq = self.alloc("b_upq", [128, 8], F32)
        b_qr = self.alloc("b_qr", [128, 8], F32)
        bfc_row = self.alloc("bfc_row", [1, D], BF16)
        xlq = self.alloc("xlq", [128, 8 * GT], BF16)
        cqT = self.alloc("cqT", [128, 4 * GT], BF16)
        qcT = self.alloc("qcT", [128, 8 * GT], BF16)
        qrT = self.alloc("qrT", [128, 8 * GT], BF16)
        qrf = self.alloc("qrf", [128, GT], F32)
        qrf2 = self.alloc("qrf2", [128, GT], F32)
        ropeAq = self.alloc("ropeAq", [128, GT], F32)
        ropeBq = self.alloc("ropeBq", [128, GT], F32)
        aoT = self.alloc("aoT", [128, 8 * GT], BF16)
        ebuf = self.alloc("ebuf", [128, 4 * GT], BF16)
        rrep = self.alloc("rrep", [128, GT], F32)
        xres = self.alloc("xres", [128, D], F32)
        xout = self.alloc("xout", [128, D], F32)

        wv_dq = self.loadw_chunks(w_dq, w["w_dqT"], NCH, DKV, "w_dq")
        wv_upq = self.loadw_chunks(w_upq, w["w_upqT"], 4, UP, "w_upq")
        wv_qr = self.loadw_chunks(w_qr, w["w_qrT"], 4, UP, "w_qr")
        wv_fc = self.loadw_chunks(w_fc, w["w_fcT"], 8, D, "w_fc")
        wv_fc = max(wv_fc, 0)
        wv_bfc = self.loadw(bfc_row[:, :], w["b_fc"][None, :], "w_fc")
        self.load_sync(b_dq[:, :], w["b_dq_c"][:, :], "s_bias")
        self.load_sync(b_upq[:, :], w["b_upq_c"][:, :], "s_bias")
        bias_v = self.load_sync(b_qr[:, :], w["b_qr_c"][:, :], "s_bias")

        self.wait("sync", "s_cache", self.val("s_cache"))

        xlq_free = None
        cq_cons = None
        qrf_free = None
        xout_free = None
        for g in range(4):
            q0 = g * GT
            lv = []
            for c in range(NCH):
                b = c % 8
                if c >= 8:
                    self.wait_tok("sync", xlq_free)
                lv.append((f"s_xlq{b}",
                           self.load_sync(xlq[:, b * GT:(b + 1) * GT],
                                          self.lncache[s, c, :, q0:q0 + GT], f"s_xlq{b}")))
            self.wait_tok("sync", qrf_free)
            self.load_sync(ropeAq[:, :], self.t_ropeA[:, q0:q0 + GT], "s_rope")
            rope_v = self.load_sync(ropeBq[:, :], self.t_ropeBs[:, q0:q0 + GT], "s_rope")
            self.wait("tensor", "w_dq", wv_dq)

            for m in range(4):
                pb = self.bank(m % 2)
                self.wait_tok("scalar", cq_cons)
                for c in range(NCH):
                    self.wait("tensor", lv[c][0], lv[c][1])
                    mark = [("t_cq", 1)] if c == NCH - 1 else ()
                    self.op("tensor", "matmul", pb[:, :],
                            w_dq[:, c * DKV + m * 128:c * DKV + (m + 1) * 128],
                            xlq[:, (c % 8) * GT:((c % 8) + 1) * GT],
                            start=(c == 0), stop=(c == NCH - 1), incs=mark)
                pv = self.val("t_cq")
                self.wait("scalar", "t_cq", pv)
                self.wait("scalar", "s_bias", bias_v)
                r = self.op("scalar", "activation", cqT[:, m * GT:(m + 1) * GT], pb[:, :],
                            AF.Identity, bias=b_dq[:, m:m + 1], incs=[("s_cq", 1)])
                self.free_bank(m % 2, "s_cq", r["s_cq"])
            xlq_free = ("t_cq", self.val("t_cq"))
            cq_done = self.val("s_cq")
            self.wait("tensor", "s_cq", cq_done)

            self.wait("tensor", "w_upq", wv_upq)
            for mk in range(8):
                pb = self.bank(2 + mk % 2)
                pv = self.mm_chain(pb[:, :],
                    [(w_upq[:, c * UP + mk * 128:c * UP + (mk + 1) * 128],
                      cqT[:, c * GT:(c + 1) * GT]) for c in range(4)],
                    incs=[("t_qc", 1)])["t_qc"]
                self.wait("scalar", "t_qc", pv)
                r = self.op("scalar", "activation", qcT[:, mk * GT:(mk + 1) * GT], pb[:, :],
                            AF.Identity, bias=b_upq[:, mk:mk + 1], incs=[("s_qc", 1)])
                self.free_bank(2 + mk % 2, "s_qc", r["s_qc"])
            for mk in range(8):
                pb = self.bank(4)
                pv = self.mm_chain(pb[:, :],
                    [(w_qr[:, c * UP + mk * 128:c * UP + (mk + 1) * 128],
                      cqT[:, c * GT:(c + 1) * GT]) for c in range(4)],
                    incs=[("t_qr", 1)])["t_qr"]
                self.wait("scalar", "t_qr", pv)
                self.wait_tok("scalar", qrf_free)
                r = self.op("scalar", "activation", qrf[:, :], pb[:, :],
                            AF.Identity, bias=b_qr[:, mk:mk + 1], incs=[("s_qr", 1)])
                self.free_bank(4, "s_qr", r["s_qr"])
                self.wait("vector", "s_qr", r["s_qr"])
                self.wait("vector", "s_rope", rope_v)
                self.op("vector", "stream_shuffle", qrf2[:, :], qrf[:, :], PAIRSWAP)
                self.op("vector", "tensor_mul", qrf[:, :], qrf[:, :], ropeAq[:, :])
                self.op("vector", "tensor_mul", qrf2[:, :], qrf2[:, :], ropeBq[:, :])
                r2 = self.op("vector", "tensor_add", qrT[:, mk * GT:(mk + 1) * GT],
                             qrf[:, :], qrf2[:, :], incs=[("v_qr", 1)])
                qrf_free = ("v_qr", r2["v_qr"])
            cq_cons = ("t_qr", self.val("t_qr"))
            self.wait("tensor", "s_qc", self.val("s_qc"))
            self.wait("tensor", "v_qr", self.val("v_qr"))

            ep_pend = None
            for j in range(NH // 2):
                pset = 2 if (j % 2) else 4
                pos = {0: self.bank(pset), 1: self.bank(pset + 1)}
                pend_v = None
                av_last = {}
                for kt in range(16):
                    cur = []
                    for ab in range(2):
                        h = 2 * j + ab
                        base = 64 * ab
                        ps = self.bank(ab)
                        self.op("tensor", "matmul", ps[:, :],
                                KcT[base:base + 64, j * S + kt * 128:j * S + (kt + 1) * 128],
                                qcT[base:base + 64, j * GT:(j + 1) * GT],
                                start=True, stop=False)
                        pv = self.op("tensor", "matmul", ps[:, :],
                                     krT[base:base + 64, kt * 128:(kt + 1) * 128],
                                     qrT[base:base + 64, j * GT:(j + 1) * GT],
                                     start=False, stop=True, incs=[("t_sc", 1)])["t_sc"]
                        ei = (kt * 2 + ab) % 4
                        eb = ebuf[:, ei * GT:(ei + 1) * GT]
                        self.wait("scalar", "t_sc", pv)
                        self.wait_tok("scalar", self._eb_free.get(ei))
                        ev = self.op("scalar", "activation", eb, ps[:, :], AF.Exp,
                                     scale=SCALE0, incs=[("s_e", 1)])["s_e"]
                        self.free_bank(ab, "s_e", ev)
                        cur.append((eb, ev, h, ei, kt))
                    if pend_v is not None:
                        for (eb, ev, h, ei, pk) in pend_v:
                            ab = h % 2
                            self.wait("tensor", "s_e", ev)
                            av = self.op("tensor", "matmul", pos[ab][0:65, :],
                                         Vaug[:, pk * 1040 + 65 * h:pk * 1040 + 65 * (h + 1)],
                                         eb, start=(pk == 0), stop=False,
                                         incs=[("t_av", 1)])["t_av"]
                            self._eb_free[ei] = ("t_av", av)
                            av_last[ab] = av
                    pend_v = cur
                for (eb, ev, h, ei, pk) in pend_v:
                    ab = h % 2
                    self.wait("tensor", "s_e", ev)
                    av = self.op("tensor", "matmul", pos[ab][0:65, :],
                                 Vaug[:, pk * 1040 + 65 * h:pk * 1040 + 65 * (h + 1)],
                                 eb, start=False, stop=True,
                                 incs=[("t_av", 1)])["t_av"]
                    self._eb_free[ei] = ("t_av", av)
                    av_last[ab] = av

                def ep_fin(j=j, pos=pos, pset=pset, av_last=dict(av_last)):
                    self.wait("vector", "t_av", av_last[1])
                    for ab in range(2):
                        r = self.op("vector", "reciprocal", rrep[0:1, :], pos[ab][64:65, :],
                                    incs=[("v_rz", 1)])
                        self.wait("tensor", "v_rz", r["v_rz"])
                        pr = self.bank(ab)
                        pv3 = self.op("tensor", "matmul", pr[0:64, :], self.ones_f[:, 0:64],
                                      rrep[0:1, :], start=True, stop=True,
                                      incs=[("t_rz", 1)])["t_rz"]
                        self.free_bank(ab, "t_rz", pv3)
                        self.wait("vector", "t_rz", pv3)
                        rc = self.op("vector", "tensor_copy", rrep[64:128, :], pr[0:64, :],
                                     incs=[("v_rr", 1)])
                        self.wait("vector", "v_rr", rc["v_rr"])
                        base = 64 * ab
                        r2 = self.op("vector", "tensor_mul",
                                     aoT[base:base + 64, j * GT:(j + 1) * GT],
                                     pos[ab][0:64, :], rrep[64:128, :], incs=[("v_ao", 1)])
                        self.free_bank(pset + ab, "v_ao", r2["v_ao"])
                if ep_pend is not None:
                    ep_pend()
                ep_pend = ep_fin
            ep_pend()
            self.wait("tensor", "v_ao", self.val("v_ao"))

            self.wait("tensor", "w_fc", wv_bfc)
            xres_free = None
            for sub in range(4):
                t0 = q0 + sub * 128
                self.wait_tok("sync", xres_free)
                xr_v = self.load_sync(xres[:, :], x_src[s, t0:t0 + 128, :], "s_xres")
                for oc in range(4):
                    py_i = oc % 2
                    py = self.bank(py_i)
                    self.wait_tok("vector", xout_free)
                    for vc in range(8):
                        self.op("tensor", "matmul", py[:, :],
                                aoT[:, vc * GT + sub * 128:vc * GT + (sub + 1) * 128],
                                w_fc[:, vc * D + oc * 512:vc * D + (oc + 1) * 512],
                                start=(vc == 0), stop=False)
                    pv = self.op("tensor", "matmul", py[:, :], self.ones_bf[:, :],
                                 bfc_row[:, oc * 512:(oc + 1) * 512], start=False, stop=True,
                                 incs=[("t_fc", 1)])["t_fc"]
                    self.wait("vector", "t_fc", pv)
                    self.wait("vector", "s_xres", xr_v)
                    r = self.op("vector", "tensor_add", xout[:, oc * 512:(oc + 1) * 512],
                                py[:, :], xres[:, oc * 512:(oc + 1) * 512],
                                incs=[("v_res", 1)])
                    self.free_bank(py_i, "v_res", r["v_res"])
                xres_free = ("v_res", self.val("v_res"))
                self.wait("sync", "v_res", self.val("v_res"))
                sv = self.store_sync(x_dst[s, t0:t0 + 128, :], xout[:, :], "s_out")
                xout_free = ("s_out", sv)

        self.freebuf(w_dq, w_upq, w_qr, w_fc, b_dq, b_upq, b_qr, bfc_row, xlq,
                     cqT, qcT, qrT, qrf, qrf2, ropeAq, ropeBq, aoT, ebuf, rrep,
                     xres, xout)


    def ffn0(self, x_src, x_dst):
        self._f0 = dict(xt_free={0: None, 1: None}, xlnT_cons=None,
                        hb_guard=None, y_guard=None, xo_guard=None,
                        gu_gate=None, wd_gate=None)
        for s in range(SPC):
            for ph in range(2):
                self.ffn0_phase(s, ph * 1024, x_src, x_dst)
        self.barrier()

    def ffn0_phase(self, s, t0, x_src, x_dst):
        w = self.win
        IC = 512
        NCK = I0 // IC
        NM = 8
        xlnT = self.alloc("xlnTf", [128, NCH * 1024], BF16)
        y_acc = self.alloc("y_acc", [128, NM * 2048], F32)
        wg = self.alloc("wg", [128, NCH * IC], BF16)
        wu = self.alloc("wu", [128, NCH * IC], BF16)
        wd = self.alloc("wd", [128, 4 * 2048], BF16)
        hb = self.alloc("hb", [128, 8 * IC], F32)
        h_s = self.alloc("h_s", [128, 4 * IC], BF16)
        h_sT = [self.alloc(f"h_sT{i}", [128, 4 * 1024], BF16) for i in range(2)]
        xt = [self.alloc(f"xtf{i}", [128, D], F32) for i in range(2)]
        ln = dict(
            st=self.alloc("lnst_ffn0", [128, 16], F32),
            sqd=y_acc[:, 0:1024].bitcast(BF16),
            xln=hb[:, 0:2048].bitcast(BF16),
            par=[0],
            guard={},
        )

        def load_gu(c):
            self.loadw_chunks(wg, w["wg0"][:, c * IC:(c + 1) * IC], NCH, IC)
            return self.loadw_chunks(wu, w["wu0"][:, c * IC:(c + 1) * IC], NCH, IC)

        def load_wd(c):
            v = 0
            for i in range(4):
                v = self.loadw(wd[:, i * 2048:(i + 1) * 2048],
                               w["wd0"][c * IC + i * 128:c * IC + (i + 1) * 128, :])
            return v

        F = self._f0
        self.wait_tok("gpsimd", F["gu_gate"])
        gu_v = load_gu(0)
        self.wait_tok("gpsimd", F["wd_gate"])
        wd_v = load_wd(0)

        xt_free = F["xt_free"]
        ln["guard"][0] = F["hb_guard"]
        self.wait_tok("scalar", F["y_guard"])
        pend = None
        for pi in range(NM // 2):
            tl = []
            for b in range(2):
                i = pi * 2 + b
                self.wait_tok("sync", xt_free[b])
                lv = self.load_sync(xt[b][:, :], x_src[s, t0 + i * 128:t0 + (i + 1) * 128, :],
                                    f"s_x{b}")
                tl.append((xt[b], (f"s_x{b}", lv), i * 128))
            if pend is not None:
                pend()
            guards = [t for t in (F["xlnT_cons"],) if t is not None]
            cons, fin = self.emit_ln_pair(ln, tl, xlnT, 1024, evac_guards=guards)
            for b in range(2):
                xt_free[b] = cons[b]
            pend = fin
        pend()
        self.wait("tensor", "v_tp", self.val("v_tp"))
        self.wait("vector", "s_ln", self.val("s_ln"))
        self.wait_tok("vector", F["y_guard"])
        self.op("vector", "memset", y_acc[:, :], 0.0)

        hg_free = {0: ("t_tp", self.val("t_tp")), 1: ("t_tp", self.val("t_tp"))}
        self.wait_tok("scalar", F["xo_guard"])
        sq_free = None
        hs_guard = {}
        tp_pend = None
        sq = xt[0][:, :]
        mx = xt[1][:, :]
        tau_cons = None
        for c in range(NCK):
            par = c % 2
            hsT = h_sT[par]
            self.wait("tensor", "g_w", gu_v)
            for mq in range(NM // 4):
                qp = (c * 2 + mq) % 2
                hbq = hb[:, qp * 4 * IC:(qp + 1) * 4 * IC]
                for sub4 in range(4):
                    m = mq * 4 + sub4
                    pg = self.bank(0)
                    tg = self.mm_chain(pg[:, :],
                        [(xlnT[:, kc * 1024 + m * 128:kc * 1024 + (m + 1) * 128],
                          wg[:, kc * IC:(kc + 1) * IC]) for kc in range(NCH)],
                        incs=[("t_g", 1)])["t_g"]
                    self.wait("scalar", "t_g", tg)
                    self.wait_tok("scalar", hg_free.get(qp))
                    sh = self.op("scalar", "activation", hbq[:, sub4 * IC:(sub4 + 1) * IC],
                                 pg[:, :], AF.Silu, incs=[("s_h", 1)])["s_h"]
                    self.free_bank(0, "s_h", sh)
                    pu = self.bank(1)
                    tu = self.mm_chain(pu[:, :],
                        [(xlnT[:, kc * 1024 + m * 128:kc * 1024 + (m + 1) * 128],
                          wu[:, kc * IC:(kc + 1) * IC]) for kc in range(NCH)],
                        incs=[("t_u", 1)])["t_u"]
                    self.wait("vector", "s_h", sh)
                    self.wait("vector", "t_u", tu)
                    self.wait_tok("vector", sq_free)
                    vh = self.op("vector", "tensor_mul", hbq[:, sub4 * IC:(sub4 + 1) * IC],
                                 hbq[:, sub4 * IC:(sub4 + 1) * IC], pu[:, :],
                                 incs=[("v_h", 1)])["v_h"]
                    self.free_bank(1, "v_h", vh)
                self.wait("scalar", "v_h", self.val("v_h"))
                W2 = 4 * IC
                sqv = None
                for qi in range(4):
                    sqv = self.op("scalar", "activation", sq[:, qi * 512:(qi + 1) * 512],
                                  hbq[:, qi:W2:4], AF.Square,
                                  incs=[("s_sq", 1)])["s_sq"]
                sq_free = ("s_sq", sqv)
                self.wait("vector", "s_sq", sqv)
                self.wait_tok("vector", tau_cons)
                s0 = sq[:, 0:512]; s1 = sq[:, 512:1024]; s2 = sq[:, 1024:1536]; s3 = sq[:, 1536:2048]
                m1 = mx[:, 0:512]; n1 = mx[:, 512:1024]; m2 = mx[:, 1024:1536]; n2 = mx[:, 1536:2048]
                tau = mx[:, 1024:1536]; keep = mx[:, 1536:2048]
                self.op("vector", "tensor_max", m1, s0, s1)
                self.op("vector", "tensor_tensor", n1, s0, s1, ALU.min)
                self.op("vector", "tensor_max", m2, s2, s3)
                self.op("vector", "tensor_tensor", n2, s2, s3, ALU.min)
                self.op("vector", "tensor_tensor", m1, m1, m2, ALU.min)
                self.op("vector", "tensor_max", n1, n1, n2)
                gt_ = self.op("vector", "tensor_tensor", tau, m1, n1, ALU.min,
                              incs=[("g_tau", 1)])["g_tau"]
                hsv = h_s[:, :]
                self.wait_tok("vector", hs_guard.get(0))
                self.wait("vector", "g_tau", gt_)
                vhs = None
                for qi, sqd in ((0, s0), (1, s1), (2, s2), (3, s3)):
                    self.op("vector", "tensor_tensor", keep, sqd, tau, ALU.is_gt)
                    vhs = self.op("vector", "tensor_mul", hsv[:, qi:W2:4],
                                  hbq[:, qi:W2:4], keep, incs=[("v_hs", 1)])["v_hs"]
                hg_free[qp] = ("v_hs", vhs)
                tau_cons = ("v_hs", vhs)

                def tp_fin(hsv=hsv, mq=mq, hsT=hsT, vhs=vhs):
                    self.wait("tensor", "v_hs", vhs)
                    ltp = None
                    for t16 in range(16):
                        isc = t16 % 4
                        mm_ = mq * 4 + t16 // 4
                        self._tpi = (self._tpi + 1) % 2
                        bi = self._tpi
                        bank = self.pt[bi]
                        self.wait_tok("tensor", self._tpev.get(bi))
                        ltp = self.op("tensor", "transpose", bank[:, 0:128],
                                      hsv[:, (t16 // 4) * IC + isc * 128:(t16 // 4) * IC + (isc + 1) * 128],
                                      self.ident_bf[:, :],
                                      incs=[("t_tp", 1)])["t_tp"]
                        self.wait("scalar", "t_tp", ltp)
                        ev = self.op("scalar", "copy",
                                     hsT[:, isc * 1024 + mm_ * 128:isc * 1024 + (mm_ + 1) * 128],
                                     bank[:, 0:128], incs=[("v_tp", 1)])
                        self._tpev[bi] = ("v_tp", ev["v_tp"])
                    hs_guard[0] = ("t_tp", ltp)
                if tp_pend is not None:
                    tp_pend()
                tp_pend = tp_fin
            if tp_pend is not None:
                tp_pend()
                tp_pend = None
            a_done_t = self.val("t_u")
            tp_done = self.val("v_tp")
            if c + 1 < NCK:
                self.wait("gpsimd", "t_u", a_done_t)
                gu_v = load_gu(c + 1)
            self.wait("tensor", "g_w", wd_v)
            self.wait("tensor", "v_tp", tp_done)
            for m in range(NM):
                for oc in range(4):
                    bi = 2 + oc
                    py = self.bank(bi)
                    ty = self.mm_chain(py[:, :],
                        [(hsT[:, isc * 1024 + m * 128:isc * 1024 + (m + 1) * 128],
                          wd[:, isc * 2048 + oc * 512:isc * 2048 + (oc + 1) * 512])
                         for isc in range(4)],
                        incs=[("t_y", 1)])["t_y"]
                    self.wait("vector", "t_y", ty)
                    ya = y_acc[:, m * 2048 + oc * 512:m * 2048 + (oc + 1) * 512]
                    r = self.op("vector", "tensor_add", ya, py[:, :], ya,
                                incs=[("v_y", 1)])
                    self.free_bank(bi, "v_y", r["v_y"])
            b_done_t = self.val("t_y")
            if c + 1 < NCK:
                self.wait("gpsimd", "t_y", b_done_t)
                wd_v = load_wd(c + 1)

        xout = h_sT[0][:, :].bitcast(F32)
        self.wait("sync", "v_hs", self.val("v_hs"))
        self.wait("sync", "s_sq", self.val("s_sq"))
        self.wait("sync", "g_tau", self.val("g_tau"))
        y_done = self.val("v_y")
        self.wait("vector", "v_y", y_done)
        xout_free = None
        for m in range(NM):
            b = m % 2
            self.wait_tok("sync", xt_free[b])
            lv = self.load_sync(xt[b][:, :], x_src[s, t0 + m * 128:t0 + (m + 1) * 128, :],
                                f"s_x{b}")
            self.wait("vector", f"s_x{b}", lv)
            self.wait_tok("vector", xout_free)
            r = self.op("vector", "tensor_add", xout,
                        y_acc[:, m * 2048:(m + 1) * 2048], xt[b][:, :],
                        incs=[("v_res", 1)])
            xt_free[b] = ("v_res", r["v_res"])
            self.wait("sync", "v_res", r["v_res"])
            sv = self.store_sync(x_dst[s, t0 + m * 128:t0 + (m + 1) * 128, :], xout,
                                 "s_out")
            xout_free = ("s_out", sv)

        F["xt_free"] = xt_free
        F["xlnT_cons"] = ("t_u", self.val("t_u"))
        F["hb_guard"] = ("v_hs", self.val("v_hs"))
        F["y_guard"] = ("v_res", self.val("v_res"))
        F["xo_guard"] = ("s_out", self.val("s_out"))
        F["gu_gate"] = ("t_u", self.val("t_u"))
        F["wd_gate"] = ("t_y", self.val("t_y"))

        self.freebuf(xlnT, y_acc, wg, wu, wd, hb, h_s,
                     h_sT[0], h_sT[1], xt[0], xt[1],
                     ln["st"])


    def l1a(self, x_src, x_dst):
        kv_sb = self.alloc("kv_sb", [128, SPC * NH * 129], BF16)
        for hf in range(2):
            self.l1a_kv(hf, x_src, kv_sb)
            self.barrier()
        for s in range(SPC):
            self.l1a_attn(s, kv_sb)
            self.barrier()
        for s in range(SPC):
            self.l1a_oproj(s, x_src, x_dst)
            self.barrier()
        self.freebuf(kv_sb)

    def l1a_kv(self, hf, x_src, kv_sb):
        GT = 512
        w = self.win
        wk = self.alloc("wk1", [128, NCH * 1024], BF16)
        wv = self.alloc("wv1", [128, NCH * 1024], BF16)
        bk_row = self.alloc("bk_row", [1, D], BF16)
        bv_row = self.alloc("bv_row", [1, D], BF16)
        xlnT = self.alloc("xlnT1", [128, NCH * GT], BF16)
        kp = [self.alloc(f"kp{i}", [128, 1024], BF16) for i in range(2)]
        vt = [self.alloc(f"vt{i}", [128, 1024], BF16) for i in range(2)]
        t2b = self.alloc("t2b", [128, GT], F32)
        eb = self.alloc("eb1", [128, GT], F32)
        xt = [self.alloc(f"xt1{i}", [128, D], F32) for i in range(2)] if hf == 0 else None
        ln = self.alloc_ln("l1a") if hf == 0 else None

        wv_k1 = self.loadw_chunks(wk, w["w_kT1"][:, hf * 1024:(hf + 1) * 1024], NCH, 1024, "w_k1")
        wv_v1 = self.loadw_chunks(wv, w["w_vT1"][:, hf * 1024:(hf + 1) * 1024], NCH, 1024, "w_v1")
        self.loadw(bk_row[:, :], w["b_k1"][None, :], "w_k1")
        wv_k1 += 16
        wv_v1b = self.loadw(bv_row[:, :], w["b_v1"][None, :], "w_v1")

        xln_cons = None
        for s in range(SPC):
            xt_free = {0: None, 1: None}
            kp_free = {0: None, 1: None}
            kv_pend = None
            cache_guard = None
            for g in range(4):
                t0 = g * GT
                if hf == 0:
                    pend = None
                    for pi in range(2):
                        tl = []
                        for b in range(2):
                            i = pi * 2 + b
                            self.wait_tok("sync", xt_free[b])
                            lv = self.load_sync(xt[b][:, :],
                                                x_src[s, t0 + i * 128:t0 + (i + 1) * 128, :],
                                                f"s_x{b}")
                            tl.append((xt[b], (f"s_x{b}", lv), i * 128))
                        guards = [t for t in (cache_guard, xln_cons) if t is not None]
                        if pend is not None:
                            pend()
                        cons, fin = self.emit_ln_pair(ln, tl, xlnT, GT, evac_guards=guards)
                        for b in range(2):
                            xt_free[b] = cons[b]
                        pend = fin
                    pend()
                    tp_done = self.val("v_tp")
                    self.wait("sync", "v_tp", tp_done)
                    cs = 0
                    for c in range(NCH):
                        cs = self.store_sync(self.lncache[s, c, :, t0:t0 + GT],
                                             xlnT[:, c * GT:(c + 1) * GT], "s_cache")
                    cache_guard = ("s_cache", cs)
                    self.wait("tensor", "v_tp", tp_done)
                else:
                    self.wait_tok("sync", xln_cons)
                    lvv = 0
                    for c in range(NCH):
                        lvv = self.load_sync(xlnT[:, c * GT:(c + 1) * GT],
                                             self.lncache[s, c, :, t0:t0 + GT], "s_cache")
                    self.wait("tensor", "s_cache", lvv)
                self.wait("tensor", "w_k1", wv_k1)
                self.wait("tensor", "w_v1", wv_v1b)

                for m in range(4):
                    st_idx = g * 4 + m
                    bp = m % 2
                    self.wait_tok("scalar", kp_free[bp])
                    for fc in range(2):
                        pb = self.bank(fc)
                        pairs = [(xlnT[:, kc * GT + m * 128:kc * GT + (m + 1) * 128],
                                  wk[:, kc * 1024 + fc * 512:kc * 1024 + (fc + 1) * 512])
                                 for kc in range(NCH)]
                        for i_, (l_, r_) in enumerate(pairs):
                            self.op("tensor", "matmul", pb[:, :], l_, r_,
                                    start=(i_ == 0), stop=False)
                        tg = self.op("tensor", "matmul", pb[:, :], self.ones_bf[:, :],
                                     bk_row[:, hf * 1024 + fc * 512:hf * 1024 + (fc + 1) * 512],
                                     start=False, stop=True, incs=[("t_k1", 1)])["t_k1"]
                        self.wait("vector", "t_k1", tg)
                        r = self.op("vector", "tensor_single_scalar", t2b[:, :], pb[:, :],
                                    0.0, ALU.min, incs=[("v_k1", 1)])
                        self.wait("scalar", "v_k1", r["v_k1"])
                        r2 = self.op("scalar", "activation", eb[:, :], t2b[:, :], AF.Exp,
                                     incs=[("s_k1", 1)])
                        self.wait("vector", "s_k1", r2["s_k1"])
                        r3 = self.op("vector", "scalar_tensor_tensor",
                                     kp[bp][:, fc * 512:(fc + 1) * 512], pb[:, :], 1.0,
                                     eb[:, :], ALU.add, ALU.max, incs=[("v_k1", 1)])
                        self.free_bank(fc, "v_k1", r3["v_k1"])
                    for fc in range(2):
                        pb = self.bank(fc)
                        pairs = [(xlnT[:, kc * GT + m * 128:kc * GT + (m + 1) * 128],
                                  wv[:, kc * 1024 + fc * 512:kc * 1024 + (fc + 1) * 512])
                                 for kc in range(NCH)]
                        for i_, (l_, r_) in enumerate(pairs):
                            self.op("tensor", "matmul", pb[:, :], l_, r_,
                                    start=(i_ == 0), stop=False)
                        tg = self.op("tensor", "matmul", pb[:, :], self.ones_bf[:, :],
                                     bv_row[:, hf * 1024 + fc * 512:hf * 1024 + (fc + 1) * 512],
                                     start=False, stop=True, incs=[("t_k1", 1)])["t_k1"]
                        self.wait("scalar", "t_k1", tg)
                        r = self.op("scalar", "activation", vt[bp][:, fc * 512:(fc + 1) * 512],
                                    pb[:, :], AF.Identity, incs=[("s_k1", 1)])
                        self.free_bank(fc, "s_k1", r["s_k1"])
                    kv_ready = self.val("s_k1")
                    kq_ready = self.val("v_k1")

                    def kv_fin(bp=bp, st_idx=st_idx, kv_ready=kv_ready, kq_ready=kq_ready):
                        self.wait("tensor", "s_k1", kv_ready)
                        self.wait("tensor", "v_k1", kq_ready)
                        tkv = None
                        for hl in range(8):
                            bi = 2 + hl // 4
                            pb = self.P[bi]
                            first = (st_idx == 0 and hl % 4 == 0)
                            if first:
                                self.wait_tok("tensor", self.bankfree.get(bi))
                            tkv = self.op("tensor", "matmul",
                                          pb[:, (hl % 4) * 128:(hl % 4 + 1) * 128],
                                          kp[bp][:, hl * 128:(hl + 1) * 128],
                                          vt[bp][:, hl * 128:(hl + 1) * 128],
                                          start=first, stop=(st_idx == 15),
                                          skip_group_check=True, incs=[("t_kv", 1)])["t_kv"]
                            pz = self.P[4]
                            if st_idx == 0 and hl == 0:
                                self.wait_tok("tensor", self.bankfree.get(4))
                            tkv = self.op("tensor", "matmul", pz[:, hl:hl + 1],
                                          kp[bp][:, hl * 128:(hl + 1) * 128],
                                          self.onescol_bf[:, :],
                                          start=(st_idx == 0 and hl == 0), stop=(st_idx == 15),
                                          skip_group_check=True, incs=[("t_kv", 1)])["t_kv"]
                        kp_free[bp] = ("t_kv", tkv)
                    if kv_pend is not None:
                        kv_pend()
                    kv_pend = kv_fin
                xln_cons = ("t_k1", self.val("t_k1"))
            if kv_pend is not None:
                kv_pend()
                kv_pend = None
            kvdone = self.val("t_kv")
            self.wait("vector", "t_kv", kvdone)
            for bi in (2, 3):
                h0 = hf * 8 + (bi - 2) * 4
                base = (s * NH + h0) * 129
                dst = kv_sb[:, base:base + 4 * 129].rearrange("p (h e) -> p h e", h=4, e=129)
                r = self.op("vector", "tensor_copy", dst[:, :, 0:128],
                            self.P[bi][:, :].rearrange("p (h e) -> p h e", h=4, e=128),
                            incs=[("v_kv", 1)])
                self.free_bank(bi, "v_kv", r["v_kv"])
            dstz = kv_sb[:, (s * NH + hf * 8) * 129:(s * NH + hf * 8 + 8) * 129]
            dstz = dstz.rearrange("p (h e) -> p h e", h=8, e=129)[:, :, 128:129]
            r = self.op("vector", "tensor_copy", dstz,
                        self.P[4][:, 0:8].rearrange("p (h o) -> p h o", h=8, o=1),
                        incs=[("v_kv", 1)])
            self.free_bank(4, "v_kv", r["v_kv"])

        fr = [wk, wv, bk_row, bv_row, xlnT, kp[0], kp[1], vt[0], vt[1], t2b, eb]
        if hf == 0:
            fr += [xt[0], xt[1]]
            self.free_ln(ln)
        self.freebuf(*fr)

    def l1a_attn(self, s, kv_sb):
        GT = 512
        w = self.win
        wq = self.alloc("wq1", [128, NCH * D], BF16)
        b_q = self.alloc("b_q1", [128, 16], F32)
        b_q1 = self.alloc("b_q1p", [128, 16], F32)
        xlq = self.alloc("xlq1", [128, 8 * GT], BF16)
        qpT = self.alloc("qpT", [128, NCH * GT], BF16)
        t2b = self.alloc("t2b1", [128, GT], F32)
        eb = self.alloc("eb2", [128, GT], F32)
        rz0 = self.alloc("rz0", [1, 2 * GT], F32)
        rrep = self.alloc("rrep1", [128, GT], F32)
        aoT = self.alloc("aoT1", [128, NH * GT], BF16)

        wv_q1 = self.loadw_chunks(wq, w["w_qT1"], NCH, D, "w_q1")
        bv = self.load_sync(b_q[:, :], w["b_q1_c"][:, :], "s_bias")
        self.wait("vector", "s_bias", bv)
        rb = self.op("vector", "tensor_scalar", b_q1[:, :], b_q[:, :], 1.0, 0.0,
                     ALU.add, ALU.add, incs=[("v_bq", 1)])["v_bq"]
        self.wait("scalar", "v_bq", rb)
        self.wait("sync", "s_cache", self.val("s_cache"))

        xlq_free = None
        qp_cons = None
        ao_guard = None
        for g in range(4):
            q0 = g * GT
            lv = []
            for c in range(NCH):
                b = c % 8
                if c >= 8:
                    self.wait_tok("sync", xlq_free)
                lv.append((f"s_xlq{b}",
                           self.load_sync(xlq[:, b * GT:(b + 1) * GT],
                                          self.lncache[s, c, :, q0:q0 + GT], f"s_xlq{b}")))
            self.wait("tensor", "w_q1", wv_q1)
            for mk in range(NCH):
                pb = self.bank(mk % 2)
                self.wait_tok("vector", qp_cons)
                for c in range(NCH):
                    self.wait("tensor", lv[c][0], lv[c][1])
                    mark = [("t_q1", 1)] if c == NCH - 1 else ()
                    self.op("tensor", "matmul", pb[:, :],
                            wq[:, c * D + mk * 128:c * D + (mk + 1) * 128],
                            xlq[:, (c % 8) * GT:((c % 8) + 1) * GT],
                            start=(c == 0), stop=(c == NCH - 1), incs=mark)
                tq = self.val("t_q1")
                self.wait("vector", "t_q1", tq)
                r = self.op("vector", "tensor_scalar", t2b[:, :], pb[:, :],
                            b_q[:, mk:mk + 1], 0.0, ALU.add, ALU.min, incs=[("v_q1", 1)])
                self.wait("scalar", "v_q1", r["v_q1"])
                r2 = self.op("scalar", "activation", eb[:, :], t2b[:, :], AF.Exp,
                             incs=[("s_q1", 1)])
                self.wait("vector", "s_q1", r2["s_q1"])
                r3 = self.op("vector", "scalar_tensor_tensor",
                             qpT[:, mk * GT:(mk + 1) * GT], pb[:, :], b_q1[:, mk:mk + 1],
                             eb[:, :], ALU.add, ALU.max, incs=[("v_q1", 1)])
                self.free_bank(mk % 2, "v_q1", r3["v_q1"])
            xlq_free = ("t_q1", self.val("t_q1"))
            self.wait("tensor", "v_q1", self.val("v_q1"))

            last_ao = None
            ep_pend = None
            for h in range(NH):
                base = (s * NH + h) * 129
                bi = 2 + h % 2
                zi = 4 + h % 2
                pa = self.bank(bi)
                ta = self.op("tensor", "matmul", pa[:, :],
                             kv_sb[:, base:base + 128], qpT[:, h * GT:(h + 1) * GT],
                             start=True, stop=True, incs=[("t_a1", 1)])["t_a1"]
                self.wait_tok("tensor", self.bankfree.get(zi))
                pz = self.P[zi]
                tz = self.op("tensor", "matmul", pz[0:1, :],
                             kv_sb[:, base + 128:base + 129], qpT[:, h * GT:(h + 1) * GT],
                             start=True, stop=True, incs=[("t_a1", 1)])["t_a1"]

                def ep_fin(h=h, pa=pa, pz=pz, bi=bi, zi=zi, tz=tz):
                    nonlocal last_ao, ao_guard
                    hb2 = h % 2
                    self.wait("vector", "t_a1", tz)
                    r = self.op("vector", "tensor_scalar", rz0[0:1, hb2 * GT:(hb2 + 1) * GT],
                                pz[0:1, :], 1e-6, 0.0, ALU.add, ALU.add, incs=[("v_a1", 1)])
                    self.wait("vector", "v_a1", r["v_a1"])
                    r2 = self.op("vector", "reciprocal", rz0[0:1, hb2 * GT:(hb2 + 1) * GT],
                                 rz0[0:1, hb2 * GT:(hb2 + 1) * GT], incs=[("v_a1", 1)])
                    self.wait("tensor", "v_a1", r2["v_a1"])
                    tr = self.op("tensor", "matmul", pz[:, :], self.ones_f[:, :],
                                 rz0[0:1, hb2 * GT:(hb2 + 1) * GT], start=True, stop=True,
                                 incs=[("t_r1", 1)])["t_r1"]
                    self.wait("vector", "t_r1", tr)
                    rc = self.op("vector", "tensor_copy", rrep[:, :], pz[:, :],
                                 incs=[("v_rr1", 1)])
                    self.free_bank(zi, "v_rr1", rc["v_rr1"])
                    self.wait("vector", "v_rr1", rc["v_rr1"])
                    self.wait_tok("vector", ao_guard)
                    ra = self.op("vector", "tensor_mul", aoT[:, h * GT:(h + 1) * GT],
                                 pa[:, :], rrep[:, :], incs=[("v_ao1", 1)])
                    self.free_bank(bi, "v_ao1", ra["v_ao1"])
                    last_ao = ra["v_ao1"]
                if ep_pend is not None:
                    ep_pend()
                ep_pend = ep_fin
            ep_pend()
            self.wait("sync", "v_ao1", last_ao)
            cs = 0
            for h in range(NH):
                cs = self.store_sync(self.aocache[s, h, :, q0:q0 + GT],
                                     aoT[:, h * GT:(h + 1) * GT], "s_ao")
            ao_guard = ("s_ao", cs)

        self.freebuf(wq, b_q, b_q1, xlq, qpT, t2b, eb, rz0, rrep, aoT)

    def l1a_oproj(self, s, x_src, x_dst):
        GT = 512
        w = self.win
        wo = self.alloc("wo1", [128, NCH * D], BF16)
        bo_row = self.alloc("bo_row", [1, D], BF16)
        aob = self.alloc("aob", [128, NCH * GT], BF16)
        xres = self.alloc("xres1", [128, D], F32)
        xout = self.alloc("xout1", [128, D], F32)
        self.loadw_chunks(wo, w["w_oT1"], NCH, D, "w_o1")
        wv_o1 = self.loadw(bo_row[:, :], w["b_o1"][None, :], "w_o1")
        self.wait("sync", "s_ao", self.val("s_ao"))

        aob_free = None
        xout_free = None
        for g in range(4):
            q0 = g * GT
            self.wait_tok("sync", aob_free)
            lvv = 0
            for c in range(NCH):
                lvv = self.load_sync(aob[:, c * GT:(c + 1) * GT],
                                     self.aocache[s, c, :, q0:q0 + GT], "s_aor")
            self.wait("tensor", "s_aor", lvv)
            self.wait("tensor", "w_o1", wv_o1)
            xres_free = None
            for sub in range(4):
                t0 = q0 + sub * 128
                self.wait_tok("sync", xres_free)
                xr_v = self.load_sync(xres[:, :], x_src[s, t0:t0 + 128, :], "s_xres")
                for oc in range(4):
                    py_i = oc % 2
                    py = self.bank(py_i)
                    self.wait_tok("vector", xout_free)
                    for vc in range(NCH):
                        self.op("tensor", "matmul", py[:, :],
                                aob[:, vc * GT + sub * 128:vc * GT + (sub + 1) * 128],
                                wo[:, vc * D + oc * 512:vc * D + (oc + 1) * 512],
                                start=(vc == 0), stop=False)
                    pv = self.op("tensor", "matmul", py[:, :], self.ones_bf[:, :],
                                 bo_row[:, oc * 512:(oc + 1) * 512], start=False, stop=True,
                                 incs=[("t_o1", 1)])["t_o1"]
                    self.wait("vector", "t_o1", pv)
                    self.wait("vector", "s_xres", xr_v)
                    r = self.op("vector", "tensor_add", xout[:, oc * 512:(oc + 1) * 512],
                                py[:, :], xres[:, oc * 512:(oc + 1) * 512],
                                incs=[("v_res", 1)])
                    self.free_bank(py_i, "v_res", r["v_res"])
                xres_free = ("v_res", self.val("v_res"))
                self.wait("sync", "v_res", self.val("v_res"))
                sv = self.store_sync(x_dst[s, t0:t0 + 128, :], xout[:, :], "s_out")
                xout_free = ("s_out", sv)
            aob_free = ("t_o1", self.val("t_o1"))
        self.freebuf(wo, bo_row, aob, xres, xout)


    def l1f(self, x_src, x_dst):
        k_shT = self.alloc("k_shT", [128, SPC * S], BF16)
        v_aug = self.alloc("v_aug", [128, SPC * 16 * 129], BF16)
        ind_all = self.alloc("ind_all", [128, SPC * 16 * 16], F32)
        vind_all = self.alloc("vind_all", [128, SPC * 16 * 16], F32)
        self.l1f_sel(0, x_src, k_shT, v_aug, ind_all, vind_all)
        self.barrier()
        self.l1f_sel(1, x_src, k_shT, v_aug, ind_all, vind_all)
        self.barrier()
        self.l1f_attn(x_src, x_dst, k_shT, v_aug)
        self.freebuf(k_shT, v_aug, ind_all, vind_all)

    def l1f_sel(self, which, x_src, k_shT, v_aug, ind_all, vind_all):
        GT = 512
        w = self.win
        wkv = self.alloc("wkv2", [128, NCH * D], BF16)
        sh_f = self.alloc("sh_f", [128, 16 * 128], F32)
        kf = self.alloc("kf2", [128, D], F32)
        km = self.alloc("km2", [128, D], F32)
        xlnT = self.alloc("xlnT2", [128, NCH * GT], BF16)
        if which == 0:
            wg2 = self.alloc("wg2", [128, NCH * 16], BF16)
            ebg = self.alloc("ebg", [128, 16], F32)
            gst = self.alloc("gst", [128, 4], F32)
            xt = [self.alloc(f"xt2{i}", [128, D], F32) for i in range(2)]
            ln = self.alloc_ln("l1f")
            wv_g2w = self.loadw_chunks(wg2, w["w_gT2"], NCH, 16, "w_g2w")
        off = 2048 if which == 0 else 4096
        wv_kv2 = self.loadw_chunks(wkv, w["w_qkvT2"][:, off:off + D], NCH, D, f"w_kv2{which}")
        if which == 1:
            self.wait("sync", "s_cache", self.val("s_cache"))

        xln_cons = None
        cache_guard = None
        kf_cons = None
        for s in range(SPC):
            xt_free = {0: None, 1: None}
            sh_cons = None
            for g in range(4):
                t0 = g * GT
                if which == 0:
                    pend = None
                    for pi in range(2):
                        tl = []
                        for b in range(2):
                            i = pi * 2 + b
                            self.wait_tok("sync", xt_free[b])
                            lv = self.load_sync(xt[b][:, :],
                                                x_src[s, t0 + i * 128:t0 + (i + 1) * 128, :],
                                                f"s_x{b}")
                            tl.append((xt[b], (f"s_x{b}", lv), i * 128))
                        guards = [t for t in (cache_guard, xln_cons) if t is not None]
                        if pend is not None:
                            pend()
                        cons, fin = self.emit_ln_pair(ln, tl, xlnT, GT, evac_guards=guards)
                        for b in range(2):
                            xt_free[b] = cons[b]
                        pend = fin
                    pend()
                    tp_done = self.val("v_tp")
                    self.wait("sync", "v_tp", tp_done)
                    cs = 0
                    for c in range(NCH):
                        cs = self.store_sync(self.lncache[s, c, :, t0:t0 + GT],
                                             xlnT[:, c * GT:(c + 1) * GT], "s_cache")
                    cache_guard = ("s_cache", cs)
                    self.wait("tensor", "v_tp", tp_done)
                else:
                    self.wait_tok("sync", xln_cons)
                    lvv = 0
                    for c in range(NCH):
                        lvv = self.load_sync(xlnT[:, c * GT:(c + 1) * GT],
                                             self.lncache[s, c, :, t0:t0 + GT], "s_cache")
                    self.wait("tensor", "s_cache", lvv)
                self.wait("tensor", f"w_kv2{which}", wv_kv2)

                for m in range(4):
                    tile = g * 4 + m
                    icol = (s * 16 + tile) * 16
                    if which == 0:
                        self.wait("tensor", "w_g2w", wv_g2w)
                        pl = self.bank(5)
                        tl = self.mm_chain(pl[:, 0:16],
                            [(xlnT[:, kc * GT + m * 128:kc * GT + (m + 1) * 128],
                              wg2[:, kc * 16:(kc + 1) * 16]) for kc in range(NCH)],
                            incs=[("t_g2", 1)])["t_g2"]
                        self.wait("vector", "t_g2", tl)
                        r = self.op("vector", "reduce_max", gst[:, 0:1], pl[:, 0:16],
                                    axis=AX.X, incs=[("v_g2", 1)])
                        self.wait("vector", "v_g2", r["v_g2"])
                        r = self.op("vector", "tensor_scalar", gst[:, 1:2], gst[:, 0:1],
                                    -1.0, 0.0, ALU.mult, ALU.add, incs=[("v_g2", 1)])
                        self.wait("vector", "v_g2", r["v_g2"])
                        ri = self.op("vector", "tensor_single_scalar",
                                     ind_all[:, icol:icol + 16], pl[:, 0:16],
                                     gst[:, 0:1], ALU.is_ge, incs=[("v_g2", 1)])
                        self.wait("scalar", "v_g2", ri["v_g2"])
                        re = self.op("scalar", "activation", ebg[:, :], pl[:, 0:16],
                                     AF.Exp, bias=gst[:, 1:2],
                                     accum_out=gst[:, 2:3], incs=[("s_g2", 1)])
                        self.free_bank(5, "s_g2", re["s_g2"])
                        self.wait("vector", "s_g2", re["s_g2"])
                        r = self.op("vector", "reciprocal", gst[:, 3:4], gst[:, 2:3],
                                    incs=[("v_g2", 1)])
                        self.wait("vector", "v_g2", r["v_g2"])
                        r = self.op("vector", "tensor_scalar",
                                    vind_all[:, icol:icol + 16],
                                    ind_all[:, icol:icol + 16], gst[:, 3:4], 0.0,
                                    ALU.mult, ALU.add, incs=[("v_g2", 1)])
                        self.wait("vector", "v_g2", r["v_g2"])
                    for fc in range(4):
                        pb = self.bank(fc % 2)
                        tg = self.mm_chain(pb[:, :],
                            [(xlnT[:, kc * GT + m * 128:kc * GT + (m + 1) * 128],
                              wkv[:, kc * D + fc * 512:kc * D + (fc + 1) * 512])
                             for kc in range(NCH)],
                            incs=[("t_k2", 1)])["t_k2"]
                        self.wait("scalar", "t_k2", tg)
                        self.wait_tok("scalar", kf_cons)
                        r = self.op("scalar", "activation", kf[:, fc * 512:(fc + 1) * 512],
                                    pb[:, :], AF.Identity, incs=[("s_k2", 1)])
                        self.free_bank(fc % 2, "s_k2", r["s_k2"])
                    self.wait("vector", "s_k2", self.val("s_k2"))
                    sel = ind_all if which == 0 else vind_all
                    indb = sel[:, icol:icol + 16][:, :, None].broadcast_to([128, 16, 128])
                    r = self.op("vector", "tensor_tensor",
                                km[:, :].rearrange("p (h dd) -> p h dd", h=16),
                                kf[:, :].rearrange("p (h dd) -> p h dd", h=16),
                                indb, ALU.mult, incs=[("v_k2", 1)])
                    self.wait("vector", "v_k2", r["v_k2"])
                    self.wait_tok("vector", sh_cons)
                    r = self.op("vector", "tensor_reduce",
                                sh_f[:, tile * 128:(tile + 1) * 128],
                                km[:, :].rearrange("p (h dd) -> p dd h", h=16),
                                axis=AX.X, op=ALU.add, incs=[("v_k2", 1)])
                    kf_cons = ("v_k2", r["v_k2"])
                xln_cons = ("t_k2", self.val("t_k2"))
            shv = self.val("v_k2")
            self.wait("vector", "v_k2", shv)
            if which == 0:
                ksb = km
                r = self.op("vector", "tensor_copy",
                            kf[:, :].bitcast(BF16)[:, 0:2048], sh_f[:, :],
                            incs=[("v_k2", 1)])
                self.wait("tensor", "v_k2", r["v_k2"])
                for t in range(16):
                    self._tpi = (self._tpi + 1) % 2
                    bi = self._tpi
                    bank = self.pt[bi]
                    self.wait_tok("tensor", self._tpev.get(bi))
                    ltp = self.op("tensor", "transpose", bank[:, 0:128],
                                  kf[:, :].bitcast(BF16)[:, t * 128:(t + 1) * 128],
                                  self.ident_bf[:, :], incs=[("t_tp", 1)])["t_tp"]
                    self.wait("vector", "t_tp", ltp)
                    ev = self.op("vector", "tensor_copy",
                                 k_shT[:, s * S + t * 128:s * S + (t + 1) * 128],
                                 bank[:, 0:128], incs=[("v_tp", 1)])
                    self._tpev[bi] = ("v_tp", ev["v_tp"])
                sh_cons = ("t_tp", self.val("t_tp"))
            else:
                dst = v_aug[:, s * 16 * 129:(s + 1) * 16 * 129]
                dstv = dst.rearrange("p (b e) -> p b e", b=16, e=129)
                r = self.op("vector", "tensor_copy", dstv[:, :, 0:128],
                            sh_f[:, :].rearrange("p (b dd) -> p b dd", b=16),
                            incs=[("v_k2", 1)])
                r = self.op("vector", "memset", dstv[:, :, 128:129], 1.0,
                            incs=[("v_k2", 1)])
                sh_cons = ("v_k2", r["v_k2"])

        fr = [wkv, sh_f, kf, km, xlnT]
        if which == 0:
            fr += [wg2, ebg, gst, xt[0], xt[1]]
            self.free_ln(ln)
        self.freebuf(*fr)

    def l1f_attn(self, x_src, x_dst, k_shT, v_aug):
        GT = 512
        w = self.win
        wq2 = self.alloc("wq2", [128, NCH * D], BF16)
        masks = self.alloc("masks2", [128, 4 * 512], BF16)
        xlq = self.alloc("xlq2", [128, 8 * GT], BF16)
        qT = self.alloc("qT2", [128, NCH * GT], BF16)
        ebr = self.alloc("ebr", [128, 2 * GT], F32)
        eb = self.alloc("eb3", [128, 4 * GT], BF16)
        ysub = [self.alloc(f"ysub{i}", [128, D], F32) for i in range(4)]
        rzb = self.alloc("rzb", [128, 4], F32)
        xres = self.alloc("xres2", [128, D], F32)
        xout = self.alloc("xout2", [128, D], F32)
        wv_q2 = self.loadw_chunks(wq2, w["w_qkvT2"][:, 0:D], NCH, D, "w_q2")
        mv = self.load_sync(masks[:, :], self.t_masks01[:, :], "s_bias")
        self.wait("sync", "s_cache", self.val("s_cache"))

        xlq_free = None
        qT_cons = None
        for s in range(SPC):
            for g in range(4):
                q0 = g * GT
                lv = []
                for c in range(NCH):
                    b = c % 8
                    if c >= 8:
                        self.wait_tok("sync", xlq_free)
                    lv.append((f"s_xlq{b}",
                               self.load_sync(xlq[:, b * GT:(b + 1) * GT],
                                              self.lncache[s, c, :, q0:q0 + GT],
                                              f"s_xlq{b}")))
                self.wait("tensor", "w_q2", wv_q2)
                for mk in range(NCH):
                    pb = self.bank(mk % 2)
                    self.wait_tok("scalar", qT_cons)
                    for c in range(NCH):
                        self.wait("tensor", lv[c][0], lv[c][1])
                        mark = [("t_q2", 1)] if c == NCH - 1 else ()
                        self.op("tensor", "matmul", pb[:, :],
                                wq2[:, c * D + mk * 128:c * D + (mk + 1) * 128],
                                xlq[:, (c % 8) * GT:((c % 8) + 1) * GT],
                                start=(c == 0), stop=(c == NCH - 1), incs=mark)
                    tq = self.val("t_q2")
                    self.wait("scalar", "t_q2", tq)
                    r = self.op("scalar", "activation", qT[:, mk * GT:(mk + 1) * GT],
                                pb[:, :], AF.Identity, incs=[("s_q2", 1)])
                    self.free_bank(mk % 2, "s_q2", r["s_q2"])
                xlq_free = ("t_q2", self.val("t_q2"))
                self.wait("tensor", "s_q2", self.val("s_q2"))

                nkt = 4 * g + 4
                for jp in range(NH // 2):
                    ybanks = {0: (self.bank(2), self.bank(3)),
                              1: (self.bank(4), self.bank(5))}
                    pend_y = None
                    for kt in range(nkt):
                        cur = []
                        for ab in range(2):
                            h = 2 * jp + ab
                            ps = self.bank(ab)
                            tsV = self.op("tensor", "matmul", ps[:, :],
                                          k_shT[:, s * S + kt * 128:s * S + (kt + 1) * 128],
                                          qT[:, h * GT:(h + 1) * GT],
                                          start=True, stop=True, incs=[("t_s2", 1)])["t_s2"]
                            ei = (kt * 2 + ab) % 4
                            ebt = eb[:, ei * GT:(ei + 1) * GT]
                            self.wait("scalar", "t_s2", tsV)
                            self.wait_tok("scalar", self._eb_free.get(("2", ei)))
                            if kt >= 4 * g:
                                ri = ab % 2
                                self.wait_tok("scalar", self._ebr_free.get(ri))
                                rr = self.op("scalar", "activation",
                                             ebr[:, ri * GT:(ri + 1) * GT], ps[:, :],
                                             AF.Exp, scale=SCALE1, incs=[("s_e2", 1)])
                                self.free_bank(ab, "s_e2", rr["s_e2"])
                                self.wait("vector", "s_e2", rr["s_e2"])
                                self.wait("vector", "s_bias", mv)
                                self.wait_tok("vector", self._eb_free.get(("2v", ei)))
                                jj = kt - 4 * g
                                ev = self.op("vector", "tensor_mul", ebt,
                                             ebr[:, ri * GT:(ri + 1) * GT],
                                             masks[:, jj * 512:(jj + 1) * 512],
                                             incs=[("v_e2", 1)])["v_e2"]
                                self._ebr_free[ri] = ("v_e2", ev)
                                cur.append((ebt, ("v_e2", ev), h, ei, kt))
                            else:
                                rr = self.op("scalar", "activation", ebt, ps[:, :],
                                             AF.Exp, scale=SCALE1, incs=[("s_e2", 1)])
                                self.free_bank(ab, "s_e2", rr["s_e2"])
                                cur.append((ebt, ("s_e2", rr["s_e2"]), h, ei, kt))
                        if pend_y is not None:
                            for (ebt_, tok_, h_, ei_, pk_) in pend_y:
                                ab_ = h_ % 2
                                self.wait_tok("tensor", tok_)
                                tv = None
                                for sub in range(4):
                                    py = ybanks[ab_][sub // 2]
                                    off = 129 * (sub % 2)
                                    tv = self.op("tensor", "matmul", py[:, off:off + 129],
                                                 ebt_[:, sub * 128:(sub + 1) * 128],
                                                 v_aug[:, (s * 16 + pk_) * 129:(s * 16 + pk_ + 1) * 129],
                                                 start=(pk_ == 0 and sub % 2 == 0), stop=False,
                                                 skip_group_check=True,
                                                 incs=[("t_y2", 1)])["t_y2"]
                                self._eb_free[("2", ei_)] = ("t_y2", tv)
                                self._eb_free[("2v", ei_)] = ("t_y2", tv)
                        pend_y = cur
                    for (ebt_, tok_, h_, ei_, pk_) in pend_y:
                        ab_ = h_ % 2
                        self.wait_tok("tensor", tok_)
                        tv = None
                        for sub in range(4):
                            py = ybanks[ab_][sub // 2]
                            off = 129 * (sub % 2)
                            tv = self.op("tensor", "matmul", py[:, off:off + 129],
                                         ebt_[:, sub * 128:(sub + 1) * 128],
                                         v_aug[:, (s * 16 + pk_) * 129:(s * 16 + pk_ + 1) * 129],
                                         start=False, stop=True,
                                         skip_group_check=True,
                                         incs=[("t_y2", 1)])["t_y2"]
                        self._eb_free[("2", ei_)] = ("t_y2", tv)
                        self._eb_free[("2v", ei_)] = ("t_y2", tv)
                    tyv = self.val("t_y2")
                    self.wait("vector", "t_y2", tyv)
                    for ab in range(2):
                        h = 2 * jp + ab
                        for sub in range(4):
                            py = ybanks[ab][sub // 2]
                            off = 129 * (sub % 2)
                            r = self.op("vector", "reciprocal", rzb[:, sub:sub + 1],
                                        py[:, off + 128:off + 129], incs=[("v_y2", 1)])
                            self.wait("vector", "v_y2", r["v_y2"])
                            r = self.op("vector", "tensor_scalar",
                                        ysub[sub][:, h * 128:(h + 1) * 128],
                                        py[:, off:off + 128], rzb[:, sub:sub + 1], 0.0,
                                        ALU.mult, ALU.add, incs=[("v_y2", 1)])
                        self.free_bank(2 + 2 * ab, "v_y2", r["v_y2"])
                        self.free_bank(3 + 2 * ab, "v_y2", r["v_y2"])
                qT_cons = ("t_s2", self.val("t_s2"))
                ydone = self.val("v_y2")
                self.wait("vector", "v_y2", ydone)
                xres_free = None
                xout_free = self._xout2_free
                for sub in range(4):
                    t0 = q0 + sub * 128
                    self.wait_tok("sync", xres_free)
                    xr_v = self.load_sync(xres[:, :], x_src[s, t0:t0 + 128, :], "s_xres")
                    self.wait("vector", "s_xres", xr_v)
                    self.wait_tok("vector", xout_free)
                    r = self.op("vector", "tensor_add", xout[:, :], ysub[sub][:, :],
                                xres[:, :], incs=[("v_res", 1)])
                    xres_free = ("v_res", r["v_res"])
                    self.wait("sync", "v_res", r["v_res"])
                    sv = self.store_sync(x_dst[s, t0:t0 + 128, :], xout[:, :], "s_out")
                    xout_free = ("s_out", sv)
                self._xout2_free = xout_free

        self.freebuf(wq2, masks, xlq, qT, ebr, eb, ysub[0], ysub[1], ysub[2],
                     ysub[3], rzb, xres, xout)

    def build(self):
        self._eb_free = {}
        self._pair_e = {}
        self.setup()
        l0a_dst = self.out if DBG_STAGE == "L0A" else self.x_a
        self.l0a(self.x_in, l0a_dst)
        self.barrier()
        if DBG_STAGE != "L0A":
            ffn_dst = self.out if DBG_STAGE == "FFN0" else self.x_b
            self.ffn0(self.x_a, ffn_dst)
            self.barrier()
        if DBG_STAGE not in ("L0A", "FFN0"):
            l1a_dst = self.out if DBG_STAGE == "L1A" else self.x_a
            self.l1a(self.x_b, l1a_dst)
            self.barrier()
        if DBG_STAGE is None:
            self._xout2_free = None
            self._ebr_free = {}
            self.l1f(self.x_a, self.out)
            self.barrier()
        self.wait("sync", "s_out", self.val("s_out"))
        return self.finalize()


def _bcols(b):
    n = b.shape[0]
    if n < 128:
        return np.ascontiguousarray(b.reshape(n, 1))
    return np.ascontiguousarray(b.reshape(n // 128, 128).T)


def _prep_shared(inputs):
    T = lambda k: np.ascontiguousarray(np.asarray(inputs[k]).T)
    C = lambda k: _bcols(np.asarray(inputs[k]))
    A = lambda k: np.ascontiguousarray(np.asarray(inputs[k]))
    return {
        "w_dkvT": T("L0_attn_W_down_kv"), "w_upkT": T("L0_attn_W_up_k"),
        "w_upvT": T("L0_attn_W_up_v"), "w_dqT": T("L0_attn_W_down_q"),
        "w_upqT": T("L0_attn_W_up_q"), "w_qrT": T("L0_attn_W_qr"),
        "w_krT": T("L0_attn_W_kr"), "w_fcT": T("L0_attn_W_fc"),
        "b_dkv_c": C("L0_attn_b_down_kv"), "b_upk_c": C("L0_attn_b_up_k"),
        "b_upv_c": C("L0_attn_b_up_v"), "b_dq_c": C("L0_attn_b_down_q"),
        "b_upq_c": C("L0_attn_b_up_q"), "b_qr_c": C("L0_attn_b_qr"),
        "b_kr_c": C("L0_attn_b_kr"), "b_fc": A("L0_attn_b_fc"),
        "wg0": A("L0_ffn_w_gate"), "wu0": A("L0_ffn_w_up"), "wd0": A("L0_ffn_w_down"),
        "w_qT1": T("L1_attn_q_w"), "w_kT1": T("L1_attn_k_w"),
        "w_vT1": T("L1_attn_v_w"), "w_oT1": T("L1_attn_o_w"),
        "b_q1_c": C("L1_attn_q_b"), "b_k1": A("L1_attn_k_b"),
        "b_v1": A("L1_attn_v_b"), "b_o1": A("L1_attn_o_b"),
        "w_gT2": T("L1_ffn_W_gate"), "w_qkvT2": T("L1_ffn_W_qkv"),
    }


def kernel(**inputs):
    shared = _prep_shared(inputs)
    x = np.ascontiguousarray(np.asarray(inputs["L0_attn_h"], dtype=np.float32))
    mk = MK({k: v.shape for k, v in shared.items()})
    nc = mk.build()
    in_maps = []
    for c in range(NCORES):
        m = dict(shared)
        m["x_in"] = np.ascontiguousarray(x[c * SPC:(c + 1) * SPC])
        in_maps.append(m)
    res = run_bass_kernel_spmd(nc, in_maps, core_ids=list(range(NCORES)))
    return np.concatenate([r["out"] for r in res.results], axis=0)


# version 11 (final)
# speedup vs baseline: 1.0233x; 1.0233x over previous
import numpy as np
import ml_dtypes
from contextlib import ExitStack

import concourse.bass as bass
import concourse.mybir as mybir
from concourse.bass_utils import run_bass_kernel_spmd

F32 = mybir.dt.float32
BF16 = mybir.dt.bfloat16
AF = mybir.ActivationFunctionType
ALU = mybir.AluOpType
AX = mybir.AxisListType
BFNP = ml_dtypes.bfloat16

S = 2048
D = 2048
SPC = 2
NCORES = 8
NCH = D // 128

DKV = 512
UP = 1024
NH = 16
RD = 64
SCALE0 = float(1.0 / np.sqrt(192.0))
I0 = 8192
HD = 128
SCALE1 = float(1.0 / np.sqrt(128.0))

ENGS = ("tensor", "vector", "scalar", "gpsimd", "sync")
SB_BASE = 16384 + 2560
PBYTES = 228352
PAIRSWAP = [i ^ 1 for i in range(32)]

DBG_STAGE = None


class MK:
    def __init__(self, wshapes):
        self.nc = bass.Bass("TRN2", target_bir_lowering=False, detect_race_conditions=False)
        self.ctx = ExitStack()
        self.ops = {e: [] for e in ENGS}
        self.sems = {}
        self.cnt = {}
        self.waited = {e: {} for e in ENGS}
        self.free = [(SB_BASE, PBYTES)]
        self.peak = 0
        self._nm = 0
        self.wshapes = wshapes
        self.bankfree = {}
        self._spans = {}
        self._tpi = 0

    def sem(self, name):
        if name not in self.sems:
            self.sems[name] = self.ctx.enter_context(self.nc.semaphore(name))
            self.cnt[name] = 0
        return self.sems[name]

    def val(self, name):
        self.sem(name)
        return self.cnt[name]

    def inc_only(self, name, amt=1):
        self.sem(name)
        self.cnt[name] += amt
        return self.cnt[name]

    def op(self, eng, meth, *a, incs=(), **kw):
        hs = [(self.sem(n), amt) for (n, amt) in incs]
        out = {}
        for (n, amt) in incs:
            self.cnt[n] += amt
            out[n] = self.cnt[n]
        def fn(e, meth=meth, a=a, kw=kw, hs=hs):
            inst = getattr(e, meth)(*a, **kw)
            for (h, amt) in hs:
                inst = inst.then_inc(h, amt)
        self.ops[eng].append(fn)
        return out

    def wait(self, eng, name, v):
        if v is None or v <= 0:
            return
        if self.waited[eng].get(name, 0) >= v:
            return
        self.waited[eng][name] = v
        h = self.sem(name)
        self.ops[eng].append(lambda e, h=h, v=v: e.wait_ge(h, v))

    def wait_tok(self, eng, tok):
        if tok is None:
            return
        if isinstance(tok, list):
            for t in tok:
                self.wait(eng, t[0], t[1])
            return
        self.wait(eng, tok[0], tok[1])

    def selfsync(self, eng):
        n = "self_" + eng
        r = self.op(eng, "nop", incs=[(n, 1)])
        self.wait(eng, n, r[n])

    def barrier(self):
        n = "bar"
        self.sem(n)
        base = self.cnt[n]
        self.op("vector", "engine_nop", incs=[(n, 1)])
        self.op("gpsimd", "engine_nop", incs=[(n, 1)])
        self.op("scalar", "activation", self.dummy[0:1, 0:1], self.dummy[0:1, 1:2],
                AF.Identity, incs=[(n, 1)])
        self.op("tensor", "nop", incs=[(n, 1)])
        self.op("sync", "nop", incs=[(n, 1)])
        for e in ENGS:
            self.wait(e, n, base + len(ENGS))

    def mm_chain(self, out_ap, pairs, incs=()):
        n = len(pairs)
        res = None
        for i, (l, r) in enumerate(pairs):
            res = self.op("tensor", "matmul", out_ap, l, r,
                          start=(i == 0), stop=(i == n - 1),
                          incs=incs if i == n - 1 else ())
        return res

    def bank(self, i):
        self.wait_tok("tensor", self.bankfree.get(i))
        return self.P[i]

    def free_bank(self, i, sem, v):
        self.bankfree[i] = (sem, v)

    def alloc(self, name, shape, dtype, align=64):
        es = mybir.dt.size(dtype)
        nb = int(np.prod(shape[1:])) * es if len(shape) > 1 else es
        nb = (nb + align - 1) // align * align
        for i, (a, b) in enumerate(self.free):
            st = (a + align - 1) // align * align
            if st + nb <= b:
                pieces = []
                if st > a:
                    pieces.append((a, st))
                if st + nb < b:
                    pieces.append((st + nb, b))
                self.free[i:i + 1] = pieces
                t = self.nc.alloc_sbuf_tensor_at(f"{name}_{self._nm}", list(shape), dtype, offset=st)
                self._nm += 1
                self._spans[id(t)] = (st, st + nb)
                self.peak = max(self.peak, st + nb)
                return t
        raise MemoryError(f"SBUF alloc failed: {name} {shape} {dtype} need {nb}B free={self.free}")

    def freebuf(self, *ts):
        for t in ts:
            self.free.append(self._spans.pop(id(t)))
        self.free.sort()
        merged = []
        for a, b in self.free:
            if merged and merged[-1][1] >= a:
                merged[-1][1] = max(merged[-1][1], b)
            else:
                merged.append([a, b])
        self.free = [tuple(x) for x in merged]

    def finalize(self):
        nc = self.nc
        with nc.Block() as block:
            for eng in ENGS:
                lst = self.ops[eng]
                getattr(block, eng)(lambda e, lst=lst: [fn(e) for fn in lst])
        self.ctx.close()
        return nc

    def loadw(self, dst, src_ap, semn="g_w"):
        return self.op("gpsimd", "dma_start", out=dst, in_=src_ap, incs=[(semn, 16)])[semn]

    def loadw_chunks(self, dst, w_ap, kchunks, cols, semn="g_w"):
        v = 0
        for c in range(kchunks):
            v = self.loadw(dst[:, c * cols:(c + 1) * cols], w_ap[c * 128:(c + 1) * 128, :], semn)
        return v

    def load_sync(self, dst, src_ap, semn="s_misc"):
        return self.op("sync", "dma_start", out=dst, in_=src_ap, incs=[(semn, 16)])[semn]

    def store_sync(self, dst_ap, src, semn="s_out"):
        return self.op("sync", "dma_start", out=dst_ap, in_=src, incs=[(semn, 16)])[semn]

    def alloc_ln(self, tag):
        return dict(
            st=self.alloc(f"lnst_{tag}", [128, 16], F32),
            sqd=self.alloc(f"lnsq_{tag}", [128, 2048], BF16),
            xln=self.alloc(f"lnx_{tag}", [128, 2 * 2048], BF16),
            par=[0],
            guard={},
        )

    def free_ln(self, ln):
        self.freebuf(ln["st"], ln["sqd"], ln["xln"])

    def emit_ln_tile(self, ln, xt, load_tok, xlnT, colbase, gt, evac_guards=()):
        st = ln["st"]
        self.wait_tok("vector", load_tok)
        self.wait_tok("scalar", load_tok)
        rs = self.op("vector", "reduce_sum", st[:, 0:1], xt[:, :], axis=AX.X,
                     incs=[("v_ln", 1)])
        rq = self.op("scalar", "activation", ln["sqd"][:, :], xt[:, :], AF.Square,
                     accum_out=st[:, 1:2], incs=[("s_ln", 1)])
        self.wait("vector", "v_ln", rs["v_ln"])
        self.wait("vector", "s_ln", rq["s_ln"])
        inv_d = 1.0 / float(D)
        def vs(meth, *a, **kw):
            r = self.op("vector", meth, *a, incs=[("v_ln", 1)], **kw)
            self.wait("vector", "v_ln", r["v_ln"])
            return r
        vs("tensor_scalar", st[:, 2:3], st[:, 0:1], inv_d, 0.0, ALU.mult, ALU.add)
        vs("tensor_mul", st[:, 3:4], st[:, 2:3], st[:, 2:3])
        vs("scalar_tensor_tensor", st[:, 4:5], st[:, 1:2], inv_d,
           st[:, 3:4], ALU.mult, ALU.subtract)
        vs("tensor_scalar", st[:, 5:6], st[:, 4:5], 1e-5, 0.0, ALU.add, ALU.add)
        r1 = vs("reciprocal", st[:, 6:7], st[:, 5:6])
        self.wait("scalar", "v_ln", r1["v_ln"])
        r2 = self.op("scalar", "activation", st[:, 7:8], st[:, 6:7], AF.Sqrt,
                     incs=[("s_ln", 1)])
        self.wait("vector", "s_ln", r2["s_ln"])
        vs("tensor_mul", st[:, 8:9], st[:, 2:3], st[:, 7:8])
        r3 = vs("tensor_scalar", st[:, 9:10], st[:, 8:9], -1.0, 0.0,
                ALU.mult, ALU.add)
        self.wait("scalar", "v_ln", r3["v_ln"])
        par = ln["par"][0]
        ln["par"][0] = 1 - par
        xln = ln["xln"][:, par * 2048:(par + 1) * 2048]
        self.wait_tok("scalar", ln["guard"].get(par))
        ap = self.op("scalar", "activation", xln, xt[:, :], AF.Identity,
                     bias=st[:, 9:10], scale=st[:, 7:8], incs=[("s_ln", 1)])

        def finish(ap=ap, xln=xln, par=par, xlnT=xlnT, colbase=colbase, gt=gt,
                   evac_guards=tuple(evac_guards)):
            self.wait("tensor", "s_ln", ap["s_ln"])
            tpv = None
            ltp = None
            for h in range(2):
                self._tpi = (self._tpi + 1) % 2
                bi = self._tpi
                bank = self.pt[bi]
                self.wait_tok("tensor", self._tpev.get(bi))
                tp = None
                for c in range(8 * h, 8 * h + 8):
                    tp = self.op("tensor", "transpose",
                                 bank[:, (c % 8) * 128:((c % 8) + 1) * 128],
                                 xln[:, c * 128:(c + 1) * 128], self.ident_bf[:, :],
                                 incs=[("t_tp", 1)])
                ltp = tp["t_tp"]
                self.wait("vector", "t_tp", ltp)
                for gtok in evac_guards:
                    self.wait_tok("vector", gtok)
                dst = xlnT[:, :].rearrange("p (c t) -> p c t", t=gt)[
                    :, 8 * h:8 * h + 8, colbase:colbase + 128]
                ev = self.op("vector", "tensor_copy", dst,
                             bank[:, :].rearrange("p (c t) -> p c t", t=128),
                             incs=[("v_tp", 1)])
                self._tpev[bi] = ("v_tp", ev["v_tp"])
                tpv = ev["v_tp"]
            ln["guard"][par] = ("t_tp", ltp)
            return ("v_tp", tpv)
        return ("s_ln", ap["s_ln"]), finish

    def emit_ln_pair(self, ln, tiles, xlnT, gt, evac_guards=()):
        st = ln["st"]
        n = len(tiles)
        rqs = []
        for i, (xt, lt, cb) in enumerate(tiles):
            self.wait_tok("vector", lt)
            self.wait_tok("scalar", lt)
            rs = self.op("vector", "reduce_sum", st[:, 0 + i:1 + i], xt[:, :], axis=AX.X,
                         incs=[("v_ln", 1)])
            rq = self.op("scalar", "activation", ln["sqd"][:, :], xt[:, :], AF.Square,
                         accum_out=st[:, 2 + i:3 + i], incs=[("s_ln", 1)])
            rqs.append(rq["s_ln"])
        self.wait("vector", "v_ln", rs["v_ln"])
        self.wait("vector", "s_ln", rq["s_ln"])
        inv_d = 1.0 / float(D)
        def vs(meth, *a, **kw):
            r = self.op("vector", meth, *a, incs=[("v_ln", 1)], **kw)
            self.wait("vector", "v_ln", r["v_ln"])
            return r
        vs("tensor_scalar", st[:, 4:4 + n], st[:, 0:n], inv_d, 0.0, ALU.mult, ALU.add)
        vs("tensor_mul", st[:, 6:6 + n], st[:, 4:4 + n], st[:, 4:4 + n])
        vs("scalar_tensor_tensor", st[:, 8:8 + n], st[:, 2:2 + n], inv_d,
           st[:, 6:6 + n], ALU.mult, ALU.subtract)
        vs("tensor_scalar", st[:, 10:10 + n], st[:, 8:8 + n], 1e-5, 0.0, ALU.add, ALU.add)
        r1 = vs("reciprocal", st[:, 12:12 + n], st[:, 10:10 + n])
        self.wait("scalar", "v_ln", r1["v_ln"])
        r2 = self.op("scalar", "activation", st[:, 14:14 + n], st[:, 12:12 + n], AF.Sqrt,
                     incs=[("s_ln", 1)])
        self.wait("vector", "s_ln", r2["s_ln"])
        vs("tensor_mul", st[:, 6:6 + n], st[:, 4:4 + n], st[:, 14:14 + n])
        r3 = vs("tensor_scalar", st[:, 4:4 + n], st[:, 6:6 + n], -1.0, 0.0,
                ALU.mult, ALU.add)
        self.wait("scalar", "v_ln", r3["v_ln"])
        self.wait_tok("scalar", ln["guard"].get(0))
        avs = []
        for i, (xt, lt, cb) in enumerate(tiles):
            ap = self.op("scalar", "activation", ln["xln"][:, i * 2048:(i + 1) * 2048],
                         xt[:, :], AF.Identity, bias=st[:, 4 + i:5 + i],
                         scale=st[:, 14 + i:15 + i], incs=[("s_ln", 1)])
            avs.append(("s_ln", ap["s_ln"]))
        cons = [[("s_ln", avs[i][1])] for i in range(n)]

        def finish(tiles=tiles, avs=avs, xlnT=xlnT, gt=gt,
                   evac_guards=tuple(evac_guards)):
            ltp = None
            tpv = None
            for i, (xt, lt, cb) in enumerate(tiles):
                self.wait("tensor", avs[i][0], avs[i][1])
                for h in range(2):
                    self._tpi = (self._tpi + 1) % 2
                    bi = self._tpi
                    bank = self.pt[bi]
                    self.wait_tok("tensor", self._tpev.get(bi))
                    for c in range(8 * h, 8 * h + 8):
                        tp = self.op("tensor", "transpose",
                                     bank[:, (c % 8) * 128:((c % 8) + 1) * 128],
                                     ln["xln"][:, i * 2048 + c * 128:i * 2048 + (c + 1) * 128],
                                     self.ident_bf[:, :], incs=[("t_tp", 1)])
                    ltp = tp["t_tp"]
                    self.wait("vector", "t_tp", ltp)
                    for gtok in evac_guards:
                        self.wait_tok("vector", gtok)
                    dst = xlnT[:, :].rearrange("p (c t) -> p c t", t=gt)[
                        :, 8 * h:8 * h + 8, cb:cb + 128]
                    ev = self.op("vector", "tensor_copy", dst,
                                 bank[:, :].rearrange("p (c t) -> p c t", t=128),
                                 incs=[("v_tp", 1)])
                    self._tpev[bi] = ("v_tp", ev["v_tp"])
                    tpv = ev["v_tp"]
            ln["guard"][0] = ("t_tp", ltp)
            return ("v_tp", tpv)
        return cons, finish

    def setup(self):
        nc = self.nc
        self.x_in = nc.dram_tensor("x_in", [SPC, S, D], F32, kind="ExternalInput").ap()
        self.out = nc.dram_tensor("out", [SPC, S, D], F32, kind="ExternalOutput").ap()
        self.x_a = nc.dram_tensor("x_a", [SPC, S, D], F32, kind="Internal").ap()
        self.x_b = nc.dram_tensor("x_b", [SPC, S, D], F32, kind="Internal").ap()
        self.lncache = nc.dram_tensor("lncache", [SPC, NCH, 128, S], BF16, kind="Internal").ap()
        self.aocache = nc.dram_tensor("aocache", [SPC, NCH, 128, S], BF16, kind="Internal").ap()
        self.win = {}
        for name, shp in self.wshapes.items():
            self.win[name] = nc.dram_tensor(name, list(shp), F32, kind="ExternalInput").ap()

        theta = 1.0 / (10000.0 ** (np.arange(0, RD, 2, dtype=np.float64) / RD))
        ang = np.arange(S, dtype=np.float64)[:, None] * theta[None, :]
        A64 = np.repeat(np.sin(ang), 2, axis=1).T
        B64 = np.repeat(np.cos(ang), 2, axis=1).T
        sign = np.where(np.arange(RD) % 2 == 0, -1.0, 1.0)[:, None]
        ropeA_np = np.concatenate([A64, A64], 0).astype(np.float32)
        ropeBs_np = np.concatenate([B64 * sign, B64 * sign], 0).astype(np.float32)
        km = np.arange(128)[:, None]
        qm = np.arange(512)[None, :]
        masks = np.concatenate(
            [np.where(km + 128 * j <= qm, 0.0, -1e9).astype(np.float32) for j in range(4)], 1)
        masks01 = np.concatenate(
            [(km + 128 * j <= qm).astype(BFNP) for j in range(4)], 1)
        self.t_ident = nc.inline_tensor(np.eye(128, dtype=BFNP), name="identb").ap()
        self.t_identf = nc.inline_tensor(np.eye(128, dtype=np.float32), name="identf").ap()
        self.t_ropeA = nc.inline_tensor(ropeA_np, name="ropeA").ap()
        self.t_ropeBs = nc.inline_tensor(ropeBs_np, name="ropeBs").ap()
        self.t_masks = nc.inline_tensor(masks, name="diagmask").ap()
        self.t_masks01 = nc.inline_tensor(masks01, name="diagmask01").ap()

        self.dummy = self.alloc("dummy", [1, 2], F32)
        self.op("scalar", "memzero", self.dummy[:, :])
        self.ident_bf = self.alloc("identb", [128, 128], BF16)
        self.ident_f = self.alloc("identf", [128, 128], F32)
        self.ones_bf = self.alloc("ones_bf", [1, 128], BF16)
        self.ones_f = self.alloc("ones_f", [1, 128], F32)
        self.onescol_bf = self.alloc("onescol_bf", [128, 1], BF16)
        self.P = [self.ctx.enter_context(self.nc.psum_tensor(f"p{i}", [128, 512], F32))
                  for i in range(6)]
        self.pt = [self.ctx.enter_context(self.nc.psum_tensor(f"pt{i}", [128, 1024], BF16))
                   for i in range(2)]
        self._tpev = {}

        gv = self.loadw(self.ident_bf[:, :], self.t_ident[:, :])
        sv = self.op("sync", "dma_start", out=self.ident_f[:, :], in_=self.t_identf[:, :],
                     incs=[("s_misc", 16)])["s_misc"]
        self.op("vector", "memset", self.ones_bf[:, :], 1.0)
        self.op("vector", "memset", self.ones_f[:, :], 1.0)
        self.op("vector", "memset", self.onescol_bf[:, :], 1.0)
        for e in ENGS:
            self.wait(e, "g_w", gv)
            self.wait(e, "s_misc", sv)
        self.barrier()

    def l0a(self, x_src, x_dst):
        KcT = self.alloc("KcT", [128, 8 * S], BF16)
        krT = self.alloc("krT", [128, S], BF16)
        Vaug = self.alloc("Vaug", [128, 16 * 1040], BF16)
        for s in range(SPC):
            self.l0a_passA(s, x_src, KcT, krT, Vaug)
            self.barrier()
            self.l0a_passB(s, x_src, x_dst, KcT, krT, Vaug)
            self.barrier()
        self.freebuf(KcT, krT, Vaug)

    def l0a_passA(self, s, x_src, KcT, krT, Vaug):
        GT = 512
        w = self.win
        w_dkv = self.alloc("w_dkv", [128, NCH * DKV], BF16)
        w_upk = self.alloc("w_upk", [128, 4 * UP], BF16)
        w_upv = self.alloc("w_upv", [128, 4 * UP], BF16)
        w_kr = self.alloc("w_kr", [128, NCH * RD], BF16)
        b_dkv = self.alloc("b_dkv", [128, 4], F32)
        b_upk = self.alloc("b_upk", [128, 8], F32)
        b_upv = self.alloc("b_upv", [128, 8], F32)
        b_kr = self.alloc("b_kr", [64, 1], F32)
        ropeAk = self.alloc("ropeAk", [64, GT], F32)
        ropeBk = self.alloc("ropeBk", [64, GT], F32)
        xt = [self.alloc(f"xt{i}", [128, D], F32) for i in range(2)]
        xlnT = self.alloc("xlnT", [128, NCH * GT], BF16)
        ckvT = self.alloc("ckvT", [128, 4 * GT], BF16)
        vtmp = self.alloc("vtmp", [128, GT], BF16)
        krf = self.alloc("krf", [64, GT], F32)
        krf2 = self.alloc("krf2", [64, GT], F32)
        ln = self.alloc_ln("l0a")

        wv_dkv = self.loadw_chunks(w_dkv, w["w_dkvT"], NCH, DKV, "w_dkv")
        wv_upk = self.loadw_chunks(w_upk, w["w_upkT"], 4, UP, "w_upk")
        wv_upv = self.loadw_chunks(w_upv, w["w_upvT"], 4, UP, "w_upv")
        wv_kr = self.loadw_chunks(w_kr, w["w_krT"], NCH, RD, "w_kr")
        self.load_sync(b_dkv[:, :], w["b_dkv_c"][:, :], "s_bias")
        self.load_sync(b_upk[:, :], w["b_upk_c"][:, :], "s_bias")
        self.load_sync(b_upv[:, :], w["b_upv_c"][:, :], "s_bias")
        bias_v = self.load_sync(b_kr[:, :], w["b_kr_c"][:, :], "s_bias")

        if True:
            self.op("vector", "memset",
                    Vaug[:, :].rearrange("p (b h e) -> p b h e", h=16, e=65)[:, :, :, 64:65],
                    1.0, incs=[("v_vau", 1)])
            ones_done = self.val("v_vau")

            xt_free = {0: None, 1: None}
            ckv_cons = None
            xlnT_cons = None
            cache_guard = None
            vtmp_free = None
            kr_cons = None
            for g in range(4):
                t0 = g * GT
                self.wait_tok("sync", kr_cons)
                self.load_sync(ropeAk[:, :], self.t_ropeA[0:64, t0:t0 + GT], "s_rope")
                ropek_v = self.load_sync(ropeBk[:, :], self.t_ropeBs[0:64, t0:t0 + GT], "s_rope")
                pend = None
                for pi in range(2):
                    tl = []
                    for b in range(2):
                        i = pi * 2 + b
                        self.wait_tok("sync", xt_free[b])
                        lv = self.load_sync(xt[b][:, :],
                                            x_src[s, t0 + i * 128:t0 + (i + 1) * 128, :],
                                            f"s_x{b}")
                        tl.append((xt[b], (f"s_x{b}", lv), i * 128))
                    guards = [t for t in (xlnT_cons, cache_guard) if t is not None]
                    if pend is not None:
                        pend()
                    cons, fin = self.emit_ln_pair(ln, tl, xlnT, GT, evac_guards=guards)
                    for b in range(2):
                        xt_free[b] = cons[b]
                    pend = fin
                pend()
                tp_done = self.val("v_tp")
                self.wait("sync", "v_tp", tp_done)
                cs = 0
                for c in range(NCH):
                    cs = self.store_sync(self.lncache[s, c, :, t0:t0 + GT],
                                         xlnT[:, c * GT:(c + 1) * GT], "s_cache")
                cache_guard = ("s_cache", cs)

                self.wait("tensor", "w_dkv", wv_dkv)
                self.wait("tensor", "v_tp", tp_done)
                for m in range(4):
                    pb = self.bank(m % 2)
                    self.wait_tok("scalar", ckv_cons)
                    pv = self.mm_chain(pb[:, :],
                        [(w_dkv[:, c * DKV + m * 128:c * DKV + (m + 1) * 128],
                          xlnT[:, c * GT:(c + 1) * GT]) for c in range(NCH)],
                        incs=[("t_ckv", 1)])["t_ckv"]
                    self.wait("vector", "t_ckv", pv)
                    self.wait("vector", "s_bias", bias_v)
                    r = self.op("vector", "tensor_scalar", ckvT[:, m * GT:(m + 1) * GT],
                                pb[:, :], b_dkv[:, m:m + 1], 0.0, ALU.add, ALU.add,
                                incs=[("s_ckv", 1)])
                    self.free_bank(m % 2, "s_ckv", r["s_ckv"])
                ckv_done = self.val("s_ckv")
                self.wait("tensor", "s_ckv", ckv_done)

                self.wait("tensor", "w_upk", wv_upk)
                for mk in range(8):
                    pb = self.bank(2 + mk % 2)
                    pv = self.mm_chain(pb[:, :],
                        [(w_upk[:, c * UP + mk * 128:c * UP + (mk + 1) * 128],
                          ckvT[:, c * GT:(c + 1) * GT]) for c in range(4)],
                        incs=[("t_kc", 1)])["t_kc"]
                    self.wait("vector", "t_kc", pv)
                    r = self.op("vector", "tensor_scalar", KcT[:, mk * S + t0:mk * S + t0 + GT],
                                pb[:, :], b_upk[:, mk:mk + 1], 0.0, ALU.add, ALU.add,
                                incs=[("s_kc", 1)])
                    self.free_bank(2 + mk % 2, "s_kc", r["s_kc"])
                self.wait("tensor", "w_upv", wv_upv)
                for mk in range(8):
                    pb = self.bank(2 + mk % 2)
                    pv = self.mm_chain(pb[:, :],
                        [(w_upv[:, c * UP + mk * 128:c * UP + (mk + 1) * 128],
                          ckvT[:, c * GT:(c + 1) * GT]) for c in range(4)],
                        incs=[("t_kc", 1)])["t_kc"]
                    self.wait("vector", "t_kc", pv)
                    self.wait_tok("vector", vtmp_free)
                    r = self.op("vector", "tensor_scalar", vtmp[:, :], pb[:, :],
                                b_upv[:, mk:mk + 1], 0.0, ALU.add, ALU.add,
                                incs=[("s_kc", 1)])
                    self.free_bank(2 + mk % 2, "s_kc", r["s_kc"])
                    self.wait("tensor", "s_kc", r["s_kc"])
                    last_tp = None
                    for sub in range(4):
                        self._tpi = (self._tpi + 1) % 2
                        bi = self._tpi
                        bank = self.pt[bi]
                        self.wait_tok("tensor", self._tpev.get(bi))
                        pv2 = self.op("tensor", "transpose", bank[:, 0:128],
                                      vtmp[:, sub * 128:(sub + 1) * 128], self.ident_bf[:, :],
                                      incs=[("t_tp", 1)])["t_tp"]
                        last_tp = pv2
                        self.wait("vector", "t_tp", pv2)
                        self.wait("vector", "v_vau", ones_done)
                        blk = g * 4 + sub
                        dst = Vaug[:, blk * 1040 + 130 * mk:blk * 1040 + 130 * (mk + 1)]
                        dst = dst.rearrange("p (h e) -> p h e", h=2, e=65)[:, :, 0:64]
                        ev = self.op("vector", "tensor_copy", dst,
                                     bank[:, 0:128].rearrange("p (h e) -> p h e", h=2, e=64),
                                     incs=[("v_tp", 1)])
                        self._tpev[bi] = ("v_tp", ev["v_tp"])
                    vtmp_free = ("t_tp", last_tp)

                self.wait("tensor", "w_kr", wv_kr)
                pb = self.bank(4)
                self.wait_tok("tensor", kr_cons)
                pv = self.mm_chain(pb[0:64, :],
                    [(w_kr[:, c * RD:(c + 1) * RD],
                      xlnT[:, c * GT:(c + 1) * GT]) for c in range(NCH)],
                    incs=[("t_kr", 1)])["t_kr"]
                self.wait("scalar", "t_kr", pv)
                r = self.op("scalar", "activation", krf[:, :], pb[0:64, :],
                            AF.Identity, bias=b_kr[:, 0:1], incs=[("s_kr", 1)])
                self.free_bank(4, "s_kr", r["s_kr"])
                self.wait("vector", "s_kr", r["s_kr"])
                self.op("vector", "stream_shuffle", krf2[:, :], krf[:, :], PAIRSWAP)
                self.wait("vector", "s_rope", ropek_v)
                self.op("vector", "tensor_mul", krf[:, :], krf[:, :], ropeAk[:, :])
                self.op("vector", "tensor_mul", krf2[:, :], krf2[:, :], ropeBk[:, :])
                self.op("vector", "tensor_add", krT[0:64, t0:t0 + GT], krf[:, :], krf2[:, :])
                rk = self.op("vector", "tensor_copy", krT[64:128, t0:t0 + GT],
                             krT[0:64, t0:t0 + GT], incs=[("v_kr", 1)])
                kr_cons = ("v_kr", rk["v_kr"])

                xlnT_cons = ("t_kr", self.val("t_kr"))
                ckv_cons = ("t_kc", self.val("t_kc"))

        self.freebuf(w_dkv, w_upk, w_upv, w_kr, b_dkv, b_upk, b_upv, b_kr,
                     ropeAk, ropeBk, xt[0], xt[1], xlnT, ckvT,
                     vtmp, krf, krf2)
        self.free_ln(ln)

    def l0a_passB(self, s, x_src, x_dst, KcT, krT, Vaug):
        GT = 512
        w = self.win
        w_dq = self.alloc("w_dq", [128, NCH * DKV], BF16)
        w_upq = self.alloc("w_upq", [128, 4 * UP], BF16)
        w_qr = self.alloc("w_qr", [128, 4 * UP], BF16)
        w_fc = self.alloc("w_fc", [128, 8 * D], BF16)
        b_dq = self.alloc("b_dq", [128, 4], F32)
        b_upq = self.alloc("b_upq", [128, 8], F32)
        b_qr = self.alloc("b_qr", [128, 8], F32)
        bfc_row = self.alloc("bfc_row", [1, D], BF16)
        xlq = self.alloc("xlq", [128, 8 * GT], BF16)
        cqT = self.alloc("cqT", [128, 4 * GT], BF16)
        qcT = self.alloc("qcT", [128, 8 * GT], BF16)
        qrT = self.alloc("qrT", [128, 8 * GT], BF16)
        qrf = self.alloc("qrf", [128, GT], F32)
        qrf2 = self.alloc("qrf2", [128, GT], F32)
        ropeAq = self.alloc("ropeAq", [128, GT], F32)
        ropeBq = self.alloc("ropeBq", [128, GT], F32)
        aoT = self.alloc("aoT", [128, 8 * GT], BF16)
        ebuf = self.alloc("ebuf", [128, 4 * GT], BF16)
        rrep = self.alloc("rrep", [128, GT], F32)
        xres = self.alloc("xres", [128, D], F32)
        xout = self.alloc("xout", [128, D], F32)

        wv_dq = self.loadw_chunks(w_dq, w["w_dqT"], NCH, DKV, "w_dq")
        wv_upq = self.loadw_chunks(w_upq, w["w_upqT"], 4, UP, "w_upq")
        wv_qr = self.loadw_chunks(w_qr, w["w_qrT"], 4, UP, "w_qr")
        wv_fc = self.loadw_chunks(w_fc, w["w_fcT"], 8, D, "w_fc")
        wv_fc = max(wv_fc, 0)
        wv_bfc = self.loadw(bfc_row[:, :], w["b_fc"][None, :], "w_fc")
        self.load_sync(b_dq[:, :], w["b_dq_c"][:, :], "s_bias")
        self.load_sync(b_upq[:, :], w["b_upq_c"][:, :], "s_bias")
        bias_v = self.load_sync(b_qr[:, :], w["b_qr_c"][:, :], "s_bias")

        self.wait("sync", "s_cache", self.val("s_cache"))

        xlq_free = None
        cq_cons = None
        qrf_free = None
        xout_free = None
        for g in range(4):
            q0 = g * GT
            lv = []
            for c in range(NCH):
                b = c % 8
                if c >= 8:
                    self.wait_tok("sync", xlq_free)
                lv.append((f"s_xlq{b}",
                           self.load_sync(xlq[:, b * GT:(b + 1) * GT],
                                          self.lncache[s, c, :, q0:q0 + GT], f"s_xlq{b}")))
            self.wait_tok("sync", qrf_free)
            self.load_sync(ropeAq[:, :], self.t_ropeA[:, q0:q0 + GT], "s_rope")
            rope_v = self.load_sync(ropeBq[:, :], self.t_ropeBs[:, q0:q0 + GT], "s_rope")
            self.wait("tensor", "w_dq", wv_dq)

            for m in range(4):
                pb = self.bank(m % 2)
                self.wait_tok("scalar", cq_cons)
                for c in range(NCH):
                    self.wait("tensor", lv[c][0], lv[c][1])
                    mark = [("t_cq", 1)] if c == NCH - 1 else ()
                    self.op("tensor", "matmul", pb[:, :],
                            w_dq[:, c * DKV + m * 128:c * DKV + (m + 1) * 128],
                            xlq[:, (c % 8) * GT:((c % 8) + 1) * GT],
                            start=(c == 0), stop=(c == NCH - 1), incs=mark)
                pv = self.val("t_cq")
                self.wait("scalar", "t_cq", pv)
                self.wait("scalar", "s_bias", bias_v)
                r = self.op("scalar", "activation", cqT[:, m * GT:(m + 1) * GT], pb[:, :],
                            AF.Identity, bias=b_dq[:, m:m + 1], incs=[("s_cq", 1)])
                self.free_bank(m % 2, "s_cq", r["s_cq"])
            xlq_free = ("t_cq", self.val("t_cq"))
            cq_done = self.val("s_cq")
            self.wait("tensor", "s_cq", cq_done)

            self.wait("tensor", "w_upq", wv_upq)
            for mk in range(8):
                pb = self.bank(2 + mk % 2)
                pv = self.mm_chain(pb[:, :],
                    [(w_upq[:, c * UP + mk * 128:c * UP + (mk + 1) * 128],
                      cqT[:, c * GT:(c + 1) * GT]) for c in range(4)],
                    incs=[("t_qc", 1)])["t_qc"]
                self.wait("scalar", "t_qc", pv)
                r = self.op("scalar", "activation", qcT[:, mk * GT:(mk + 1) * GT], pb[:, :],
                            AF.Identity, bias=b_upq[:, mk:mk + 1], incs=[("s_qc", 1)])
                self.free_bank(2 + mk % 2, "s_qc", r["s_qc"])
            for mk in range(8):
                pb = self.bank(4)
                pv = self.mm_chain(pb[:, :],
                    [(w_qr[:, c * UP + mk * 128:c * UP + (mk + 1) * 128],
                      cqT[:, c * GT:(c + 1) * GT]) for c in range(4)],
                    incs=[("t_qr", 1)])["t_qr"]
                self.wait("scalar", "t_qr", pv)
                self.wait_tok("scalar", qrf_free)
                r = self.op("scalar", "activation", qrf[:, :], pb[:, :],
                            AF.Identity, bias=b_qr[:, mk:mk + 1], incs=[("s_qr", 1)])
                self.free_bank(4, "s_qr", r["s_qr"])
                self.wait("vector", "s_qr", r["s_qr"])
                self.wait("vector", "s_rope", rope_v)
                self.op("vector", "stream_shuffle", qrf2[:, :], qrf[:, :], PAIRSWAP)
                self.op("vector", "tensor_mul", qrf[:, :], qrf[:, :], ropeAq[:, :])
                self.op("vector", "tensor_mul", qrf2[:, :], qrf2[:, :], ropeBq[:, :])
                r2 = self.op("vector", "tensor_add", qrT[:, mk * GT:(mk + 1) * GT],
                             qrf[:, :], qrf2[:, :], incs=[("v_qr", 1)])
                qrf_free = ("v_qr", r2["v_qr"])
            cq_cons = ("t_qr", self.val("t_qr"))
            self.wait("tensor", "s_qc", self.val("s_qc"))
            self.wait("tensor", "v_qr", self.val("v_qr"))

            ep_pend = None
            for j in range(NH // 2):
                pset = 2 if (j % 2) else 4
                pos = {0: self.bank(pset), 1: self.bank(pset + 1)}
                pend_v = None
                av_last = {}
                for kt in range(16):
                    cur = []
                    for ab in range(2):
                        h = 2 * j + ab
                        base = 64 * ab
                        ps = self.bank(ab)
                        self.op("tensor", "matmul", ps[:, :],
                                KcT[base:base + 64, j * S + kt * 128:j * S + (kt + 1) * 128],
                                qcT[base:base + 64, j * GT:(j + 1) * GT],
                                start=True, stop=False)
                        pv = self.op("tensor", "matmul", ps[:, :],
                                     krT[base:base + 64, kt * 128:(kt + 1) * 128],
                                     qrT[base:base + 64, j * GT:(j + 1) * GT],
                                     start=False, stop=True, incs=[("t_sc", 1)])["t_sc"]
                        ei = (kt * 2 + ab) % 4
                        eb = ebuf[:, ei * GT:(ei + 1) * GT]
                        self.wait("scalar", "t_sc", pv)
                        self.wait_tok("scalar", self._eb_free.get(ei))
                        ev = self.op("scalar", "activation", eb, ps[:, :], AF.Exp,
                                     scale=SCALE0, incs=[("s_e", 1)])["s_e"]
                        self.free_bank(ab, "s_e", ev)
                        cur.append((eb, ev, h, ei, kt))
                    if pend_v is not None:
                        for (eb, ev, h, ei, pk) in pend_v:
                            ab = h % 2
                            self.wait("tensor", "s_e", ev)
                            av = self.op("tensor", "matmul", pos[ab][0:65, :],
                                         Vaug[:, pk * 1040 + 65 * h:pk * 1040 + 65 * (h + 1)],
                                         eb, start=(pk == 0), stop=False,
                                         incs=[("t_av", 1)])["t_av"]
                            self._eb_free[ei] = ("t_av", av)
                            av_last[ab] = av
                    pend_v = cur
                for (eb, ev, h, ei, pk) in pend_v:
                    ab = h % 2
                    self.wait("tensor", "s_e", ev)
                    av = self.op("tensor", "matmul", pos[ab][0:65, :],
                                 Vaug[:, pk * 1040 + 65 * h:pk * 1040 + 65 * (h + 1)],
                                 eb, start=False, stop=True,
                                 incs=[("t_av", 1)])["t_av"]
                    self._eb_free[ei] = ("t_av", av)
                    av_last[ab] = av

                def ep_fin(j=j, pos=pos, pset=pset, av_last=dict(av_last)):
                    self.wait("vector", "t_av", av_last[1])
                    for ab in range(2):
                        r = self.op("vector", "reciprocal", rrep[0:1, :], pos[ab][64:65, :],
                                    incs=[("v_rz", 1)])
                        self.wait("tensor", "v_rz", r["v_rz"])
                        pr = self.bank(ab)
                        pv3 = self.op("tensor", "matmul", pr[0:64, :], self.ones_f[:, 0:64],
                                      rrep[0:1, :], start=True, stop=True,
                                      incs=[("t_rz", 1)])["t_rz"]
                        self.free_bank(ab, "t_rz", pv3)
                        self.wait("vector", "t_rz", pv3)
                        rc = self.op("vector", "tensor_copy", rrep[64:128, :], pr[0:64, :],
                                     incs=[("v_rr", 1)])
                        self.wait("vector", "v_rr", rc["v_rr"])
                        base = 64 * ab
                        r2 = self.op("vector", "tensor_mul",
                                     aoT[base:base + 64, j * GT:(j + 1) * GT],
                                     pos[ab][0:64, :], rrep[64:128, :], incs=[("v_ao", 1)])
                        self.free_bank(pset + ab, "v_ao", r2["v_ao"])
                if ep_pend is not None:
                    ep_pend()
                ep_pend = ep_fin
            ep_pend()
            self.wait("tensor", "v_ao", self.val("v_ao"))

            self.wait("tensor", "w_fc", wv_bfc)
            xres_free = None
            for sub in range(4):
                t0 = q0 + sub * 128
                self.wait_tok("sync", xres_free)
                xr_v = self.load_sync(xres[:, :], x_src[s, t0:t0 + 128, :], "s_xres")
                for oc in range(4):
                    py_i = oc % 2
                    py = self.bank(py_i)
                    self.wait_tok("vector", xout_free)
                    for vc in range(8):
                        self.op("tensor", "matmul", py[:, :],
                                aoT[:, vc * GT + sub * 128:vc * GT + (sub + 1) * 128],
                                w_fc[:, vc * D + oc * 512:vc * D + (oc + 1) * 512],
                                start=(vc == 0), stop=False)
                    pv = self.op("tensor", "matmul", py[:, :], self.ones_bf[:, :],
                                 bfc_row[:, oc * 512:(oc + 1) * 512], start=False, stop=True,
                                 incs=[("t_fc", 1)])["t_fc"]
                    self.wait("vector", "t_fc", pv)
                    self.wait("vector", "s_xres", xr_v)
                    r = self.op("vector", "tensor_add", xout[:, oc * 512:(oc + 1) * 512],
                                py[:, :], xres[:, oc * 512:(oc + 1) * 512],
                                incs=[("v_res", 1)])
                    self.free_bank(py_i, "v_res", r["v_res"])
                xres_free = ("v_res", self.val("v_res"))
                self.wait("sync", "v_res", self.val("v_res"))
                sv = self.store_sync(x_dst[s, t0:t0 + 128, :], xout[:, :], "s_out")
                xout_free = ("s_out", sv)

        self.freebuf(w_dq, w_upq, w_qr, w_fc, b_dq, b_upq, b_qr, bfc_row, xlq,
                     cqT, qcT, qrT, qrf, qrf2, ropeAq, ropeBq, aoT, ebuf, rrep,
                     xres, xout)


    def ffn0(self, x_src, x_dst):
        for s in range(SPC):
            for ph in range(2):
                self.ffn0_phase(s, ph * 1024, x_src, x_dst)
                self.barrier()

    def ffn0_phase(self, s, t0, x_src, x_dst):
        w = self.win
        IC = 512
        NCK = I0 // IC
        NM = 8
        xlnT = self.alloc("xlnTf", [128, NCH * 1024], BF16)
        y_acc = self.alloc("y_acc", [128, NM * 2048], F32)
        wg = self.alloc("wg", [128, NCH * IC], BF16)
        wu = self.alloc("wu", [128, NCH * IC], BF16)
        wd = self.alloc("wd", [128, 4 * 2048], BF16)
        hb = self.alloc("hb", [128, 8 * IC], F32)
        h_s = self.alloc("h_s", [128, 4 * IC], BF16)
        h_sT = [self.alloc(f"h_sT{i}", [128, 4 * 1024], BF16) for i in range(2)]
        xt = [self.alloc(f"xtf{i}", [128, D], F32) for i in range(2)]
        ln = dict(
            st=self.alloc("lnst_ffn0", [128, 16], F32),
            sqd=y_acc[:, 0:1024].bitcast(BF16),
            xln=hb[:, 0:2048].bitcast(BF16),
            par=[0],
            guard={},
        )

        def load_gu(c):
            self.loadw_chunks(wg, w["wg0"][:, c * IC:(c + 1) * IC], NCH, IC)
            return self.loadw_chunks(wu, w["wu0"][:, c * IC:(c + 1) * IC], NCH, IC)

        def load_wd(c):
            v = 0
            for i in range(4):
                v = self.loadw(wd[:, i * 2048:(i + 1) * 2048],
                               w["wd0"][c * IC + i * 128:c * IC + (i + 1) * 128, :])
            return v

        gu_v = load_gu(0)
        wd_v = load_wd(0)

        xt_free = {0: None, 1: None}
        pend = None
        for pi in range(NM // 2):
            tl = []
            for b in range(2):
                i = pi * 2 + b
                self.wait_tok("sync", xt_free[b])
                lv = self.load_sync(xt[b][:, :], x_src[s, t0 + i * 128:t0 + (i + 1) * 128, :],
                                    f"s_x{b}")
                tl.append((xt[b], (f"s_x{b}", lv), i * 128))
            if pend is not None:
                pend()
            cons, fin = self.emit_ln_pair(ln, tl, xlnT, 1024)
            for b in range(2):
                xt_free[b] = cons[b]
            pend = fin
        pend()
        self.wait("tensor", "v_tp", self.val("v_tp"))
        self.wait("vector", "s_ln", self.val("s_ln"))
        self.op("vector", "memset", y_acc[:, :], 0.0)

        hg_free = {0: ("t_tp", self.val("t_tp")), 1: ("t_tp", self.val("t_tp"))}
        sq_free = None
        hs_guard = {}
        tp_pend = None
        sq = xt[0][:, :]
        mx = xt[1][:, :]
        tau_cons = None
        for c in range(NCK):
            par = c % 2
            hsT = h_sT[par]
            self.wait("tensor", "g_w", gu_v)
            for mq in range(NM // 4):
                qp = (c * 2 + mq) % 2
                hbq = hb[:, qp * 4 * IC:(qp + 1) * 4 * IC]
                for sub4 in range(4):
                    m = mq * 4 + sub4
                    pg = self.bank(0)
                    tg = self.mm_chain(pg[:, :],
                        [(xlnT[:, kc * 1024 + m * 128:kc * 1024 + (m + 1) * 128],
                          wg[:, kc * IC:(kc + 1) * IC]) for kc in range(NCH)],
                        incs=[("t_g", 1)])["t_g"]
                    self.wait("scalar", "t_g", tg)
                    self.wait_tok("scalar", hg_free.get(qp))
                    sh = self.op("scalar", "activation", hbq[:, sub4 * IC:(sub4 + 1) * IC],
                                 pg[:, :], AF.Silu, incs=[("s_h", 1)])["s_h"]
                    self.free_bank(0, "s_h", sh)
                    pu = self.bank(1)
                    tu = self.mm_chain(pu[:, :],
                        [(xlnT[:, kc * 1024 + m * 128:kc * 1024 + (m + 1) * 128],
                          wu[:, kc * IC:(kc + 1) * IC]) for kc in range(NCH)],
                        incs=[("t_u", 1)])["t_u"]
                    self.wait("vector", "s_h", sh)
                    self.wait("vector", "t_u", tu)
                    self.wait_tok("vector", sq_free)
                    vh = self.op("vector", "tensor_mul", hbq[:, sub4 * IC:(sub4 + 1) * IC],
                                 hbq[:, sub4 * IC:(sub4 + 1) * IC], pu[:, :],
                                 incs=[("v_h", 1)])["v_h"]
                    self.free_bank(1, "v_h", vh)
                self.wait("scalar", "v_h", self.val("v_h"))
                W2 = 4 * IC
                sqv = None
                for qi in range(4):
                    sqv = self.op("scalar", "activation", sq[:, qi * 512:(qi + 1) * 512],
                                  hbq[:, qi:W2:4], AF.Square,
                                  incs=[("s_sq", 1)])["s_sq"]
                sq_free = ("s_sq", sqv)
                self.wait("vector", "s_sq", sqv)
                self.wait_tok("vector", tau_cons)
                s0 = sq[:, 0:512]; s1 = sq[:, 512:1024]; s2 = sq[:, 1024:1536]; s3 = sq[:, 1536:2048]
                m1 = mx[:, 0:512]; n1 = mx[:, 512:1024]; m2 = mx[:, 1024:1536]; n2 = mx[:, 1536:2048]
                tau = mx[:, 1024:1536]; keep = mx[:, 1536:2048]
                self.op("vector", "tensor_max", m1, s0, s1)
                self.op("vector", "tensor_tensor", n1, s0, s1, ALU.min)
                self.op("vector", "tensor_max", m2, s2, s3)
                self.op("vector", "tensor_tensor", n2, s2, s3, ALU.min)
                self.op("vector", "tensor_tensor", m1, m1, m2, ALU.min)
                self.op("vector", "tensor_max", n1, n1, n2)
                gt_ = self.op("vector", "tensor_tensor", tau, m1, n1, ALU.min,
                              incs=[("g_tau", 1)])["g_tau"]
                hsv = h_s[:, :]
                self.wait_tok("vector", hs_guard.get(0))
                self.wait("vector", "g_tau", gt_)
                vhs = None
                for qi, sqd in ((0, s0), (1, s1), (2, s2), (3, s3)):
                    self.op("vector", "tensor_tensor", keep, sqd, tau, ALU.is_gt)
                    vhs = self.op("vector", "tensor_mul", hsv[:, qi:W2:4],
                                  hbq[:, qi:W2:4], keep, incs=[("v_hs", 1)])["v_hs"]
                hg_free[qp] = ("v_hs", vhs)
                tau_cons = ("v_hs", vhs)

                def tp_fin(hsv=hsv, mq=mq, hsT=hsT, vhs=vhs):
                    self.wait("tensor", "v_hs", vhs)
                    ltp = None
                    for t16 in range(16):
                        isc = t16 % 4
                        mm_ = mq * 4 + t16 // 4
                        self._tpi = (self._tpi + 1) % 2
                        bi = self._tpi
                        bank = self.pt[bi]
                        self.wait_tok("tensor", self._tpev.get(bi))
                        ltp = self.op("tensor", "transpose", bank[:, 0:128],
                                      hsv[:, (t16 // 4) * IC + isc * 128:(t16 // 4) * IC + (isc + 1) * 128],
                                      self.ident_bf[:, :],
                                      incs=[("t_tp", 1)])["t_tp"]
                        self.wait("scalar", "t_tp", ltp)
                        ev = self.op("scalar", "copy",
                                     hsT[:, isc * 1024 + mm_ * 128:isc * 1024 + (mm_ + 1) * 128],
                                     bank[:, 0:128], incs=[("v_tp", 1)])
                        self._tpev[bi] = ("v_tp", ev["v_tp"])
                    hs_guard[0] = ("t_tp", ltp)
                if tp_pend is not None:
                    tp_pend()
                tp_pend = tp_fin
            if tp_pend is not None:
                tp_pend()
                tp_pend = None
            a_done_t = self.val("t_u")
            tp_done = self.val("v_tp")
            if c + 1 < NCK:
                self.wait("gpsimd", "t_u", a_done_t)
                gu_v = load_gu(c + 1)
            self.wait("tensor", "g_w", wd_v)
            self.wait("tensor", "v_tp", tp_done)
            for m in range(NM):
                for oc in range(4):
                    bi = 2 + oc
                    py = self.bank(bi)
                    ty = self.mm_chain(py[:, :],
                        [(hsT[:, isc * 1024 + m * 128:isc * 1024 + (m + 1) * 128],
                          wd[:, isc * 2048 + oc * 512:isc * 2048 + (oc + 1) * 512])
                         for isc in range(4)],
                        incs=[("t_y", 1)])["t_y"]
                    self.wait("vector", "t_y", ty)
                    ya = y_acc[:, m * 2048 + oc * 512:m * 2048 + (oc + 1) * 512]
                    r = self.op("vector", "tensor_add", ya, py[:, :], ya,
                                incs=[("v_y", 1)])
                    self.free_bank(bi, "v_y", r["v_y"])
            b_done_t = self.val("t_y")
            if c + 1 < NCK:
                self.wait("gpsimd", "t_y", b_done_t)
                wd_v = load_wd(c + 1)

        xout = h_sT[0][:, :].bitcast(F32)
        self.wait("sync", "v_hs", self.val("v_hs"))
        self.wait("sync", "s_sq", self.val("s_sq"))
        y_done = self.val("v_y")
        self.wait("vector", "v_y", y_done)
        xout_free = None
        for m in range(NM):
            b = m % 2
            self.wait_tok("sync", xt_free[b])
            lv = self.load_sync(xt[b][:, :], x_src[s, t0 + m * 128:t0 + (m + 1) * 128, :],
                                f"s_x{b}")
            self.wait("vector", f"s_x{b}", lv)
            self.wait_tok("vector", xout_free)
            r = self.op("vector", "tensor_add", xout,
                        y_acc[:, m * 2048:(m + 1) * 2048], xt[b][:, :],
                        incs=[("v_res", 1)])
            xt_free[b] = ("v_res", r["v_res"])
            self.wait("sync", "v_res", r["v_res"])
            sv = self.store_sync(x_dst[s, t0 + m * 128:t0 + (m + 1) * 128, :], xout,
                                 "s_out")
            xout_free = ("s_out", sv)

        self.freebuf(xlnT, y_acc, wg, wu, wd, hb, h_s,
                     h_sT[0], h_sT[1], xt[0], xt[1],
                     ln["st"])


    def l1a(self, x_src, x_dst):
        kv_sb = self.alloc("kv_sb", [128, SPC * NH * 129], BF16)
        for hf in range(2):
            self.l1a_kv(hf, x_src, kv_sb)
            self.barrier()
        for s in range(SPC):
            self.l1a_attn(s, kv_sb)
            self.barrier()
        for s in range(SPC):
            self.l1a_oproj(s, x_src, x_dst)
            self.barrier()
        self.freebuf(kv_sb)

    def l1a_kv(self, hf, x_src, kv_sb):
        GT = 512
        w = self.win
        wk = self.alloc("wk1", [128, NCH * 1024], BF16)
        wv = self.alloc("wv1", [128, NCH * 1024], BF16)
        bk_row = self.alloc("bk_row", [1, D], BF16)
        bv_row = self.alloc("bv_row", [1, D], BF16)
        xlnT = self.alloc("xlnT1", [128, NCH * GT], BF16)
        kp = [self.alloc(f"kp{i}", [128, 1024], BF16) for i in range(2)]
        vt = [self.alloc(f"vt{i}", [128, 1024], BF16) for i in range(2)]
        t2b = self.alloc("t2b", [128, GT], F32)
        eb = self.alloc("eb1", [128, GT], F32)
        xt = [self.alloc(f"xt1{i}", [128, D], F32) for i in range(2)] if hf == 0 else None
        ln = self.alloc_ln("l1a") if hf == 0 else None

        wv_k1 = self.loadw_chunks(wk, w["w_kT1"][:, hf * 1024:(hf + 1) * 1024], NCH, 1024, "w_k1")
        wv_v1 = self.loadw_chunks(wv, w["w_vT1"][:, hf * 1024:(hf + 1) * 1024], NCH, 1024, "w_v1")
        self.loadw(bk_row[:, :], w["b_k1"][None, :], "w_k1")
        wv_k1 += 16
        wv_v1b = self.loadw(bv_row[:, :], w["b_v1"][None, :], "w_v1")

        xln_cons = None
        for s in range(SPC):
            xt_free = {0: None, 1: None}
            kp_free = {0: None, 1: None}
            kv_pend = None
            cache_guard = None
            for g in range(4):
                t0 = g * GT
                if hf == 0:
                    pend = None
                    for pi in range(2):
                        tl = []
                        for b in range(2):
                            i = pi * 2 + b
                            self.wait_tok("sync", xt_free[b])
                            lv = self.load_sync(xt[b][:, :],
                                                x_src[s, t0 + i * 128:t0 + (i + 1) * 128, :],
                                                f"s_x{b}")
                            tl.append((xt[b], (f"s_x{b}", lv), i * 128))
                        guards = [t for t in (cache_guard, xln_cons) if t is not None]
                        if pend is not None:
                            pend()
                        cons, fin = self.emit_ln_pair(ln, tl, xlnT, GT, evac_guards=guards)
                        for b in range(2):
                            xt_free[b] = cons[b]
                        pend = fin
                    pend()
                    tp_done = self.val("v_tp")
                    self.wait("sync", "v_tp", tp_done)
                    cs = 0
                    for c in range(NCH):
                        cs = self.store_sync(self.lncache[s, c, :, t0:t0 + GT],
                                             xlnT[:, c * GT:(c + 1) * GT], "s_cache")
                    cache_guard = ("s_cache", cs)
                    self.wait("tensor", "v_tp", tp_done)
                else:
                    self.wait_tok("sync", xln_cons)
                    lvv = 0
                    for c in range(NCH):
                        lvv = self.load_sync(xlnT[:, c * GT:(c + 1) * GT],
                                             self.lncache[s, c, :, t0:t0 + GT], "s_cache")
                    self.wait("tensor", "s_cache", lvv)
                self.wait("tensor", "w_k1", wv_k1)
                self.wait("tensor", "w_v1", wv_v1b)

                for m in range(4):
                    st_idx = g * 4 + m
                    bp = m % 2
                    self.wait_tok("scalar", kp_free[bp])
                    for fc in range(2):
                        pb = self.bank(fc)
                        pairs = [(xlnT[:, kc * GT + m * 128:kc * GT + (m + 1) * 128],
                                  wk[:, kc * 1024 + fc * 512:kc * 1024 + (fc + 1) * 512])
                                 for kc in range(NCH)]
                        for i_, (l_, r_) in enumerate(pairs):
                            self.op("tensor", "matmul", pb[:, :], l_, r_,
                                    start=(i_ == 0), stop=False)
                        tg = self.op("tensor", "matmul", pb[:, :], self.ones_bf[:, :],
                                     bk_row[:, hf * 1024 + fc * 512:hf * 1024 + (fc + 1) * 512],
                                     start=False, stop=True, incs=[("t_k1", 1)])["t_k1"]
                        self.wait("vector", "t_k1", tg)
                        r = self.op("vector", "tensor_single_scalar", t2b[:, :], pb[:, :],
                                    0.0, ALU.min, incs=[("v_k1", 1)])
                        self.wait("scalar", "v_k1", r["v_k1"])
                        r2 = self.op("scalar", "activation", eb[:, :], t2b[:, :], AF.Exp,
                                     incs=[("s_k1", 1)])
                        self.wait("vector", "s_k1", r2["s_k1"])
                        r3 = self.op("vector", "scalar_tensor_tensor",
                                     kp[bp][:, fc * 512:(fc + 1) * 512], pb[:, :], 1.0,
                                     eb[:, :], ALU.add, ALU.max, incs=[("v_k1", 1)])
                        self.free_bank(fc, "v_k1", r3["v_k1"])
                    for fc in range(2):
                        pb = self.bank(fc)
                        pairs = [(xlnT[:, kc * GT + m * 128:kc * GT + (m + 1) * 128],
                                  wv[:, kc * 1024 + fc * 512:kc * 1024 + (fc + 1) * 512])
                                 for kc in range(NCH)]
                        for i_, (l_, r_) in enumerate(pairs):
                            self.op("tensor", "matmul", pb[:, :], l_, r_,
                                    start=(i_ == 0), stop=False)
                        tg = self.op("tensor", "matmul", pb[:, :], self.ones_bf[:, :],
                                     bv_row[:, hf * 1024 + fc * 512:hf * 1024 + (fc + 1) * 512],
                                     start=False, stop=True, incs=[("t_k1", 1)])["t_k1"]
                        self.wait("scalar", "t_k1", tg)
                        r = self.op("scalar", "activation", vt[bp][:, fc * 512:(fc + 1) * 512],
                                    pb[:, :], AF.Identity, incs=[("s_k1", 1)])
                        self.free_bank(fc, "s_k1", r["s_k1"])
                    kv_ready = self.val("s_k1")
                    kq_ready = self.val("v_k1")

                    def kv_fin(bp=bp, st_idx=st_idx, kv_ready=kv_ready, kq_ready=kq_ready):
                        self.wait("tensor", "s_k1", kv_ready)
                        self.wait("tensor", "v_k1", kq_ready)
                        tkv = None
                        for hl in range(8):
                            bi = 2 + hl // 4
                            pb = self.P[bi]
                            first = (st_idx == 0 and hl % 4 == 0)
                            if first:
                                self.wait_tok("tensor", self.bankfree.get(bi))
                            tkv = self.op("tensor", "matmul",
                                          pb[:, (hl % 4) * 128:(hl % 4 + 1) * 128],
                                          kp[bp][:, hl * 128:(hl + 1) * 128],
                                          vt[bp][:, hl * 128:(hl + 1) * 128],
                                          start=first, stop=(st_idx == 15),
                                          skip_group_check=True, incs=[("t_kv", 1)])["t_kv"]
                            pz = self.P[4]
                            if st_idx == 0 and hl == 0:
                                self.wait_tok("tensor", self.bankfree.get(4))
                            tkv = self.op("tensor", "matmul", pz[:, hl:hl + 1],
                                          kp[bp][:, hl * 128:(hl + 1) * 128],
                                          self.onescol_bf[:, :],
                                          start=(st_idx == 0 and hl == 0), stop=(st_idx == 15),
                                          skip_group_check=True, incs=[("t_kv", 1)])["t_kv"]
                        kp_free[bp] = ("t_kv", tkv)
                    if kv_pend is not None:
                        kv_pend()
                    kv_pend = kv_fin
                xln_cons = ("t_k1", self.val("t_k1"))
            if kv_pend is not None:
                kv_pend()
                kv_pend = None
            kvdone = self.val("t_kv")
            self.wait("vector", "t_kv", kvdone)
            for bi in (2, 3):
                h0 = hf * 8 + (bi - 2) * 4
                base = (s * NH + h0) * 129
                dst = kv_sb[:, base:base + 4 * 129].rearrange("p (h e) -> p h e", h=4, e=129)
                r = self.op("vector", "tensor_copy", dst[:, :, 0:128],
                            self.P[bi][:, :].rearrange("p (h e) -> p h e", h=4, e=128),
                            incs=[("v_kv", 1)])
                self.free_bank(bi, "v_kv", r["v_kv"])
            dstz = kv_sb[:, (s * NH + hf * 8) * 129:(s * NH + hf * 8 + 8) * 129]
            dstz = dstz.rearrange("p (h e) -> p h e", h=8, e=129)[:, :, 128:129]
            r = self.op("vector", "tensor_copy", dstz,
                        self.P[4][:, 0:8].rearrange("p (h o) -> p h o", h=8, o=1),
                        incs=[("v_kv", 1)])
            self.free_bank(4, "v_kv", r["v_kv"])

        fr = [wk, wv, bk_row, bv_row, xlnT, kp[0], kp[1], vt[0], vt[1], t2b, eb]
        if hf == 0:
            fr += [xt[0], xt[1]]
            self.free_ln(ln)
        self.freebuf(*fr)

    def l1a_attn(self, s, kv_sb):
        GT = 512
        w = self.win
        wq = self.alloc("wq1", [128, NCH * D], BF16)
        b_q = self.alloc("b_q1", [128, 16], F32)
        b_q1 = self.alloc("b_q1p", [128, 16], F32)
        xlq = self.alloc("xlq1", [128, 8 * GT], BF16)
        qpT = self.alloc("qpT", [128, NCH * GT], BF16)
        t2b = self.alloc("t2b1", [128, GT], F32)
        eb = self.alloc("eb2", [128, GT], F32)
        rz0 = self.alloc("rz0", [1, 2 * GT], F32)
        rrep = self.alloc("rrep1", [128, GT], F32)
        aoT = self.alloc("aoT1", [128, NH * GT], BF16)

        wv_q1 = self.loadw_chunks(wq, w["w_qT1"], NCH, D, "w_q1")
        bv = self.load_sync(b_q[:, :], w["b_q1_c"][:, :], "s_bias")
        self.wait("vector", "s_bias", bv)
        rb = self.op("vector", "tensor_scalar", b_q1[:, :], b_q[:, :], 1.0, 0.0,
                     ALU.add, ALU.add, incs=[("v_bq", 1)])["v_bq"]
        self.wait("scalar", "v_bq", rb)
        self.wait("sync", "s_cache", self.val("s_cache"))

        xlq_free = None
        qp_cons = None
        ao_guard = None
        for g in range(4):
            q0 = g * GT
            lv = []
            for c in range(NCH):
                b = c % 8
                if c >= 8:
                    self.wait_tok("sync", xlq_free)
                lv.append((f"s_xlq{b}",
                           self.load_sync(xlq[:, b * GT:(b + 1) * GT],
                                          self.lncache[s, c, :, q0:q0 + GT], f"s_xlq{b}")))
            self.wait("tensor", "w_q1", wv_q1)
            for mk in range(NCH):
                pb = self.bank(mk % 2)
                self.wait_tok("vector", qp_cons)
                for c in range(NCH):
                    self.wait("tensor", lv[c][0], lv[c][1])
                    mark = [("t_q1", 1)] if c == NCH - 1 else ()
                    self.op("tensor", "matmul", pb[:, :],
                            wq[:, c * D + mk * 128:c * D + (mk + 1) * 128],
                            xlq[:, (c % 8) * GT:((c % 8) + 1) * GT],
                            start=(c == 0), stop=(c == NCH - 1), incs=mark)
                tq = self.val("t_q1")
                self.wait("vector", "t_q1", tq)
                r = self.op("vector", "tensor_scalar", t2b[:, :], pb[:, :],
                            b_q[:, mk:mk + 1], 0.0, ALU.add, ALU.min, incs=[("v_q1", 1)])
                self.wait("scalar", "v_q1", r["v_q1"])
                r2 = self.op("scalar", "activation", eb[:, :], t2b[:, :], AF.Exp,
                             incs=[("s_q1", 1)])
                self.wait("vector", "s_q1", r2["s_q1"])
                r3 = self.op("vector", "scalar_tensor_tensor",
                             qpT[:, mk * GT:(mk + 1) * GT], pb[:, :], b_q1[:, mk:mk + 1],
                             eb[:, :], ALU.add, ALU.max, incs=[("v_q1", 1)])
                self.free_bank(mk % 2, "v_q1", r3["v_q1"])
            xlq_free = ("t_q1", self.val("t_q1"))
            self.wait("tensor", "v_q1", self.val("v_q1"))

            last_ao = None
            ep_pend = None
            for h in range(NH):
                base = (s * NH + h) * 129
                bi = 2 + h % 2
                zi = 4 + h % 2
                pa = self.bank(bi)
                ta = self.op("tensor", "matmul", pa[:, :],
                             kv_sb[:, base:base + 128], qpT[:, h * GT:(h + 1) * GT],
                             start=True, stop=True, incs=[("t_a1", 1)])["t_a1"]
                self.wait_tok("tensor", self.bankfree.get(zi))
                pz = self.P[zi]
                tz = self.op("tensor", "matmul", pz[0:1, :],
                             kv_sb[:, base + 128:base + 129], qpT[:, h * GT:(h + 1) * GT],
                             start=True, stop=True, incs=[("t_a1", 1)])["t_a1"]

                def ep_fin(h=h, pa=pa, pz=pz, bi=bi, zi=zi, tz=tz):
                    nonlocal last_ao, ao_guard
                    hb2 = h % 2
                    self.wait("vector", "t_a1", tz)
                    r = self.op("vector", "tensor_scalar", rz0[0:1, hb2 * GT:(hb2 + 1) * GT],
                                pz[0:1, :], 1e-6, 0.0, ALU.add, ALU.add, incs=[("v_a1", 1)])
                    self.wait("vector", "v_a1", r["v_a1"])
                    r2 = self.op("vector", "reciprocal", rz0[0:1, hb2 * GT:(hb2 + 1) * GT],
                                 rz0[0:1, hb2 * GT:(hb2 + 1) * GT], incs=[("v_a1", 1)])
                    self.wait("tensor", "v_a1", r2["v_a1"])
                    tr = self.op("tensor", "matmul", pz[:, :], self.ones_f[:, :],
                                 rz0[0:1, hb2 * GT:(hb2 + 1) * GT], start=True, stop=True,
                                 incs=[("t_r1", 1)])["t_r1"]
                    self.wait("vector", "t_r1", tr)
                    rc = self.op("vector", "tensor_copy", rrep[:, :], pz[:, :],
                                 incs=[("v_rr1", 1)])
                    self.free_bank(zi, "v_rr1", rc["v_rr1"])
                    self.wait("vector", "v_rr1", rc["v_rr1"])
                    self.wait_tok("vector", ao_guard)
                    ra = self.op("vector", "tensor_mul", aoT[:, h * GT:(h + 1) * GT],
                                 pa[:, :], rrep[:, :], incs=[("v_ao1", 1)])
                    self.free_bank(bi, "v_ao1", ra["v_ao1"])
                    last_ao = ra["v_ao1"]
                if ep_pend is not None:
                    ep_pend()
                ep_pend = ep_fin
            ep_pend()
            self.wait("sync", "v_ao1", last_ao)
            cs = 0
            for h in range(NH):
                cs = self.store_sync(self.aocache[s, h, :, q0:q0 + GT],
                                     aoT[:, h * GT:(h + 1) * GT], "s_ao")
            ao_guard = ("s_ao", cs)

        self.freebuf(wq, b_q, b_q1, xlq, qpT, t2b, eb, rz0, rrep, aoT)

    def l1a_oproj(self, s, x_src, x_dst):
        GT = 512
        w = self.win
        wo = self.alloc("wo1", [128, NCH * D], BF16)
        bo_row = self.alloc("bo_row", [1, D], BF16)
        aob = self.alloc("aob", [128, NCH * GT], BF16)
        xres = self.alloc("xres1", [128, D], F32)
        xout = self.alloc("xout1", [128, D], F32)
        self.loadw_chunks(wo, w["w_oT1"], NCH, D, "w_o1")
        wv_o1 = self.loadw(bo_row[:, :], w["b_o1"][None, :], "w_o1")
        self.wait("sync", "s_ao", self.val("s_ao"))

        aob_free = None
        xout_free = None
        for g in range(4):
            q0 = g * GT
            self.wait_tok("sync", aob_free)
            lvv = 0
            for c in range(NCH):
                lvv = self.load_sync(aob[:, c * GT:(c + 1) * GT],
                                     self.aocache[s, c, :, q0:q0 + GT], "s_aor")
            self.wait("tensor", "s_aor", lvv)
            self.wait("tensor", "w_o1", wv_o1)
            xres_free = None
            for sub in range(4):
                t0 = q0 + sub * 128
                self.wait_tok("sync", xres_free)
                xr_v = self.load_sync(xres[:, :], x_src[s, t0:t0 + 128, :], "s_xres")
                for oc in range(4):
                    py_i = oc % 2
                    py = self.bank(py_i)
                    self.wait_tok("vector", xout_free)
                    for vc in range(NCH):
                        self.op("tensor", "matmul", py[:, :],
                                aob[:, vc * GT + sub * 128:vc * GT + (sub + 1) * 128],
                                wo[:, vc * D + oc * 512:vc * D + (oc + 1) * 512],
                                start=(vc == 0), stop=False)
                    pv = self.op("tensor", "matmul", py[:, :], self.ones_bf[:, :],
                                 bo_row[:, oc * 512:(oc + 1) * 512], start=False, stop=True,
                                 incs=[("t_o1", 1)])["t_o1"]
                    self.wait("vector", "t_o1", pv)
                    self.wait("vector", "s_xres", xr_v)
                    r = self.op("vector", "tensor_add", xout[:, oc * 512:(oc + 1) * 512],
                                py[:, :], xres[:, oc * 512:(oc + 1) * 512],
                                incs=[("v_res", 1)])
                    self.free_bank(py_i, "v_res", r["v_res"])
                xres_free = ("v_res", self.val("v_res"))
                self.wait("sync", "v_res", self.val("v_res"))
                sv = self.store_sync(x_dst[s, t0:t0 + 128, :], xout[:, :], "s_out")
                xout_free = ("s_out", sv)
            aob_free = ("t_o1", self.val("t_o1"))
        self.freebuf(wo, bo_row, aob, xres, xout)


    def l1f(self, x_src, x_dst):
        k_shT = self.alloc("k_shT", [128, SPC * S], BF16)
        v_aug = self.alloc("v_aug", [128, SPC * 16 * 129], BF16)
        ind_all = self.alloc("ind_all", [128, SPC * 16 * 16], F32)
        vind_all = self.alloc("vind_all", [128, SPC * 16 * 16], F32)
        self.l1f_sel(0, x_src, k_shT, v_aug, ind_all, vind_all)
        self.barrier()
        self.l1f_sel(1, x_src, k_shT, v_aug, ind_all, vind_all)
        self.barrier()
        self.l1f_attn(x_src, x_dst, k_shT, v_aug)
        self.freebuf(k_shT, v_aug, ind_all, vind_all)

    def l1f_sel(self, which, x_src, k_shT, v_aug, ind_all, vind_all):
        GT = 512
        w = self.win
        wkv = self.alloc("wkv2", [128, NCH * D], BF16)
        sh_f = self.alloc("sh_f", [128, 16 * 128], F32)
        kf = self.alloc("kf2", [128, D], F32)
        km = self.alloc("km2", [128, D], F32)
        xlnT = self.alloc("xlnT2", [128, NCH * GT], BF16)
        if which == 0:
            wg2 = self.alloc("wg2", [128, NCH * 16], BF16)
            ebg = self.alloc("ebg", [128, 16], F32)
            gst = self.alloc("gst", [128, 4], F32)
            xt = [self.alloc(f"xt2{i}", [128, D], F32) for i in range(2)]
            ln = self.alloc_ln("l1f")
            wv_g2w = self.loadw_chunks(wg2, w["w_gT2"], NCH, 16, "w_g2w")
        off = 2048 if which == 0 else 4096
        wv_kv2 = self.loadw_chunks(wkv, w["w_qkvT2"][:, off:off + D], NCH, D, f"w_kv2{which}")
        if which == 1:
            self.wait("sync", "s_cache", self.val("s_cache"))

        xln_cons = None
        cache_guard = None
        kf_cons = None
        for s in range(SPC):
            xt_free = {0: None, 1: None}
            sh_cons = None
            for g in range(4):
                t0 = g * GT
                if which == 0:
                    pend = None
                    for pi in range(2):
                        tl = []
                        for b in range(2):
                            i = pi * 2 + b
                            self.wait_tok("sync", xt_free[b])
                            lv = self.load_sync(xt[b][:, :],
                                                x_src[s, t0 + i * 128:t0 + (i + 1) * 128, :],
                                                f"s_x{b}")
                            tl.append((xt[b], (f"s_x{b}", lv), i * 128))
                        guards = [t for t in (cache_guard, xln_cons) if t is not None]
                        if pend is not None:
                            pend()
                        cons, fin = self.emit_ln_pair(ln, tl, xlnT, GT, evac_guards=guards)
                        for b in range(2):
                            xt_free[b] = cons[b]
                        pend = fin
                    pend()
                    tp_done = self.val("v_tp")
                    self.wait("sync", "v_tp", tp_done)
                    cs = 0
                    for c in range(NCH):
                        cs = self.store_sync(self.lncache[s, c, :, t0:t0 + GT],
                                             xlnT[:, c * GT:(c + 1) * GT], "s_cache")
                    cache_guard = ("s_cache", cs)
                    self.wait("tensor", "v_tp", tp_done)
                else:
                    self.wait_tok("sync", xln_cons)
                    lvv = 0
                    for c in range(NCH):
                        lvv = self.load_sync(xlnT[:, c * GT:(c + 1) * GT],
                                             self.lncache[s, c, :, t0:t0 + GT], "s_cache")
                    self.wait("tensor", "s_cache", lvv)
                self.wait("tensor", f"w_kv2{which}", wv_kv2)

                for m in range(4):
                    tile = g * 4 + m
                    icol = (s * 16 + tile) * 16
                    if which == 0:
                        self.wait("tensor", "w_g2w", wv_g2w)
                        pl = self.bank(5)
                        tl = self.mm_chain(pl[:, 0:16],
                            [(xlnT[:, kc * GT + m * 128:kc * GT + (m + 1) * 128],
                              wg2[:, kc * 16:(kc + 1) * 16]) for kc in range(NCH)],
                            incs=[("t_g2", 1)])["t_g2"]
                        self.wait("vector", "t_g2", tl)
                        r = self.op("vector", "reduce_max", gst[:, 0:1], pl[:, 0:16],
                                    axis=AX.X, incs=[("v_g2", 1)])
                        self.wait("vector", "v_g2", r["v_g2"])
                        r = self.op("vector", "tensor_scalar", gst[:, 1:2], gst[:, 0:1],
                                    -1.0, 0.0, ALU.mult, ALU.add, incs=[("v_g2", 1)])
                        self.wait("vector", "v_g2", r["v_g2"])
                        ri = self.op("vector", "tensor_single_scalar",
                                     ind_all[:, icol:icol + 16], pl[:, 0:16],
                                     gst[:, 0:1], ALU.is_ge, incs=[("v_g2", 1)])
                        self.wait("scalar", "v_g2", ri["v_g2"])
                        re = self.op("scalar", "activation", ebg[:, :], pl[:, 0:16],
                                     AF.Exp, bias=gst[:, 1:2],
                                     accum_out=gst[:, 2:3], incs=[("s_g2", 1)])
                        self.free_bank(5, "s_g2", re["s_g2"])
                        self.wait("vector", "s_g2", re["s_g2"])
                        r = self.op("vector", "reciprocal", gst[:, 3:4], gst[:, 2:3],
                                    incs=[("v_g2", 1)])
                        self.wait("vector", "v_g2", r["v_g2"])
                        r = self.op("vector", "tensor_scalar",
                                    vind_all[:, icol:icol + 16],
                                    ind_all[:, icol:icol + 16], gst[:, 3:4], 0.0,
                                    ALU.mult, ALU.add, incs=[("v_g2", 1)])
                        self.wait("vector", "v_g2", r["v_g2"])
                    for fc in range(4):
                        pb = self.bank(fc % 2)
                        tg = self.mm_chain(pb[:, :],
                            [(xlnT[:, kc * GT + m * 128:kc * GT + (m + 1) * 128],
                              wkv[:, kc * D + fc * 512:kc * D + (fc + 1) * 512])
                             for kc in range(NCH)],
                            incs=[("t_k2", 1)])["t_k2"]
                        self.wait("scalar", "t_k2", tg)
                        self.wait_tok("scalar", kf_cons)
                        r = self.op("scalar", "activation", kf[:, fc * 512:(fc + 1) * 512],
                                    pb[:, :], AF.Identity, incs=[("s_k2", 1)])
                        self.free_bank(fc % 2, "s_k2", r["s_k2"])
                    self.wait("vector", "s_k2", self.val("s_k2"))
                    sel = ind_all if which == 0 else vind_all
                    indb = sel[:, icol:icol + 16][:, :, None].broadcast_to([128, 16, 128])
                    r = self.op("vector", "tensor_tensor",
                                km[:, :].rearrange("p (h dd) -> p h dd", h=16),
                                kf[:, :].rearrange("p (h dd) -> p h dd", h=16),
                                indb, ALU.mult, incs=[("v_k2", 1)])
                    self.wait("vector", "v_k2", r["v_k2"])
                    self.wait_tok("vector", sh_cons)
                    r = self.op("vector", "tensor_reduce",
                                sh_f[:, tile * 128:(tile + 1) * 128],
                                km[:, :].rearrange("p (h dd) -> p dd h", h=16),
                                axis=AX.X, op=ALU.add, incs=[("v_k2", 1)])
                    kf_cons = ("v_k2", r["v_k2"])
                xln_cons = ("t_k2", self.val("t_k2"))
            shv = self.val("v_k2")
            self.wait("vector", "v_k2", shv)
            if which == 0:
                ksb = km
                r = self.op("vector", "tensor_copy",
                            kf[:, :].bitcast(BF16)[:, 0:2048], sh_f[:, :],
                            incs=[("v_k2", 1)])
                self.wait("tensor", "v_k2", r["v_k2"])
                for t in range(16):
                    self._tpi = (self._tpi + 1) % 2
                    bi = self._tpi
                    bank = self.pt[bi]
                    self.wait_tok("tensor", self._tpev.get(bi))
                    ltp = self.op("tensor", "transpose", bank[:, 0:128],
                                  kf[:, :].bitcast(BF16)[:, t * 128:(t + 1) * 128],
                                  self.ident_bf[:, :], incs=[("t_tp", 1)])["t_tp"]
                    self.wait("vector", "t_tp", ltp)
                    ev = self.op("vector", "tensor_copy",
                                 k_shT[:, s * S + t * 128:s * S + (t + 1) * 128],
                                 bank[:, 0:128], incs=[("v_tp", 1)])
                    self._tpev[bi] = ("v_tp", ev["v_tp"])
                sh_cons = ("t_tp", self.val("t_tp"))
            else:
                dst = v_aug[:, s * 16 * 129:(s + 1) * 16 * 129]
                dstv = dst.rearrange("p (b e) -> p b e", b=16, e=129)
                r = self.op("vector", "tensor_copy", dstv[:, :, 0:128],
                            sh_f[:, :].rearrange("p (b dd) -> p b dd", b=16),
                            incs=[("v_k2", 1)])
                r = self.op("vector", "memset", dstv[:, :, 128:129], 1.0,
                            incs=[("v_k2", 1)])
                sh_cons = ("v_k2", r["v_k2"])

        fr = [wkv, sh_f, kf, km, xlnT]
        if which == 0:
            fr += [wg2, ebg, gst, xt[0], xt[1]]
            self.free_ln(ln)
        self.freebuf(*fr)

    def l1f_attn(self, x_src, x_dst, k_shT, v_aug):
        GT = 512
        w = self.win
        wq2 = self.alloc("wq2", [128, NCH * D], BF16)
        masks = self.alloc("masks2", [128, 4 * 512], BF16)
        xlq = self.alloc("xlq2", [128, 8 * GT], BF16)
        qT = self.alloc("qT2", [128, NCH * GT], BF16)
        ebr = self.alloc("ebr", [128, 2 * GT], F32)
        eb = self.alloc("eb3", [128, 4 * GT], BF16)
        ysub = [self.alloc(f"ysub{i}", [128, D], F32) for i in range(4)]
        rzb = self.alloc("rzb", [128, 4], F32)
        xres = self.alloc("xres2", [128, D], F32)
        xout = self.alloc("xout2", [128, D], F32)
        wv_q2 = self.loadw_chunks(wq2, w["w_qkvT2"][:, 0:D], NCH, D, "w_q2")
        mv = self.load_sync(masks[:, :], self.t_masks01[:, :], "s_bias")
        self.wait("sync", "s_cache", self.val("s_cache"))

        xlq_free = None
        qT_cons = None
        for s in range(SPC):
            for g in range(4):
                q0 = g * GT
                lv = []
                for c in range(NCH):
                    b = c % 8
                    if c >= 8:
                        self.wait_tok("sync", xlq_free)
                    lv.append((f"s_xlq{b}",
                               self.load_sync(xlq[:, b * GT:(b + 1) * GT],
                                              self.lncache[s, c, :, q0:q0 + GT],
                                              f"s_xlq{b}")))
                self.wait("tensor", "w_q2", wv_q2)
                for mk in range(NCH):
                    pb = self.bank(mk % 2)
                    self.wait_tok("scalar", qT_cons)
                    for c in range(NCH):
                        self.wait("tensor", lv[c][0], lv[c][1])
                        mark = [("t_q2", 1)] if c == NCH - 1 else ()
                        self.op("tensor", "matmul", pb[:, :],
                                wq2[:, c * D + mk * 128:c * D + (mk + 1) * 128],
                                xlq[:, (c % 8) * GT:((c % 8) + 1) * GT],
                                start=(c == 0), stop=(c == NCH - 1), incs=mark)
                    tq = self.val("t_q2")
                    self.wait("scalar", "t_q2", tq)
                    r = self.op("scalar", "activation", qT[:, mk * GT:(mk + 1) * GT],
                                pb[:, :], AF.Identity, incs=[("s_q2", 1)])
                    self.free_bank(mk % 2, "s_q2", r["s_q2"])
                xlq_free = ("t_q2", self.val("t_q2"))
                self.wait("tensor", "s_q2", self.val("s_q2"))

                nkt = 4 * g + 4
                for jp in range(NH // 2):
                    ybanks = {0: (self.bank(2), self.bank(3)),
                              1: (self.bank(4), self.bank(5))}
                    pend_y = None
                    for kt in range(nkt):
                        cur = []
                        for ab in range(2):
                            h = 2 * jp + ab
                            ps = self.bank(ab)
                            tsV = self.op("tensor", "matmul", ps[:, :],
                                          k_shT[:, s * S + kt * 128:s * S + (kt + 1) * 128],
                                          qT[:, h * GT:(h + 1) * GT],
                                          start=True, stop=True, incs=[("t_s2", 1)])["t_s2"]
                            ei = (kt * 2 + ab) % 4
                            ebt = eb[:, ei * GT:(ei + 1) * GT]
                            self.wait("scalar", "t_s2", tsV)
                            self.wait_tok("scalar", self._eb_free.get(("2", ei)))
                            if kt >= 4 * g:
                                ri = ab % 2
                                self.wait_tok("scalar", self._ebr_free.get(ri))
                                rr = self.op("scalar", "activation",
                                             ebr[:, ri * GT:(ri + 1) * GT], ps[:, :],
                                             AF.Exp, scale=SCALE1, incs=[("s_e2", 1)])
                                self.free_bank(ab, "s_e2", rr["s_e2"])
                                self.wait("vector", "s_e2", rr["s_e2"])
                                self.wait("vector", "s_bias", mv)
                                self.wait_tok("vector", self._eb_free.get(("2v", ei)))
                                jj = kt - 4 * g
                                ev = self.op("vector", "tensor_mul", ebt,
                                             ebr[:, ri * GT:(ri + 1) * GT],
                                             masks[:, jj * 512:(jj + 1) * 512],
                                             incs=[("v_e2", 1)])["v_e2"]
                                self._ebr_free[ri] = ("v_e2", ev)
                                cur.append((ebt, ("v_e2", ev), h, ei, kt))
                            else:
                                rr = self.op("scalar", "activation", ebt, ps[:, :],
                                             AF.Exp, scale=SCALE1, incs=[("s_e2", 1)])
                                self.free_bank(ab, "s_e2", rr["s_e2"])
                                cur.append((ebt, ("s_e2", rr["s_e2"]), h, ei, kt))
                        if pend_y is not None:
                            for (ebt_, tok_, h_, ei_, pk_) in pend_y:
                                ab_ = h_ % 2
                                self.wait_tok("tensor", tok_)
                                tv = None
                                for sub in range(4):
                                    py = ybanks[ab_][sub // 2]
                                    off = 129 * (sub % 2)
                                    tv = self.op("tensor", "matmul", py[:, off:off + 129],
                                                 ebt_[:, sub * 128:(sub + 1) * 128],
                                                 v_aug[:, (s * 16 + pk_) * 129:(s * 16 + pk_ + 1) * 129],
                                                 start=(pk_ == 0 and sub % 2 == 0), stop=False,
                                                 skip_group_check=True,
                                                 incs=[("t_y2", 1)])["t_y2"]
                                self._eb_free[("2", ei_)] = ("t_y2", tv)
                                self._eb_free[("2v", ei_)] = ("t_y2", tv)
                        pend_y = cur
                    for (ebt_, tok_, h_, ei_, pk_) in pend_y:
                        ab_ = h_ % 2
                        self.wait_tok("tensor", tok_)
                        tv = None
                        for sub in range(4):
                            py = ybanks[ab_][sub // 2]
                            off = 129 * (sub % 2)
                            tv = self.op("tensor", "matmul", py[:, off:off + 129],
                                         ebt_[:, sub * 128:(sub + 1) * 128],
                                         v_aug[:, (s * 16 + pk_) * 129:(s * 16 + pk_ + 1) * 129],
                                         start=False, stop=True,
                                         skip_group_check=True,
                                         incs=[("t_y2", 1)])["t_y2"]
                        self._eb_free[("2", ei_)] = ("t_y2", tv)
                        self._eb_free[("2v", ei_)] = ("t_y2", tv)
                    tyv = self.val("t_y2")
                    self.wait("vector", "t_y2", tyv)
                    for ab in range(2):
                        h = 2 * jp + ab
                        for sub in range(4):
                            py = ybanks[ab][sub // 2]
                            off = 129 * (sub % 2)
                            r = self.op("vector", "reciprocal", rzb[:, sub:sub + 1],
                                        py[:, off + 128:off + 129], incs=[("v_y2", 1)])
                            self.wait("vector", "v_y2", r["v_y2"])
                            r = self.op("vector", "tensor_scalar",
                                        ysub[sub][:, h * 128:(h + 1) * 128],
                                        py[:, off:off + 128], rzb[:, sub:sub + 1], 0.0,
                                        ALU.mult, ALU.add, incs=[("v_y2", 1)])
                        self.free_bank(2 + 2 * ab, "v_y2", r["v_y2"])
                        self.free_bank(3 + 2 * ab, "v_y2", r["v_y2"])
                qT_cons = ("t_s2", self.val("t_s2"))
                ydone = self.val("v_y2")
                self.wait("vector", "v_y2", ydone)
                xres_free = None
                xout_free = self._xout2_free
                for sub in range(4):
                    t0 = q0 + sub * 128
                    self.wait_tok("sync", xres_free)
                    xr_v = self.load_sync(xres[:, :], x_src[s, t0:t0 + 128, :], "s_xres")
                    self.wait("vector", "s_xres", xr_v)
                    self.wait_tok("vector", xout_free)
                    r = self.op("vector", "tensor_add", xout[:, :], ysub[sub][:, :],
                                xres[:, :], incs=[("v_res", 1)])
                    xres_free = ("v_res", r["v_res"])
                    self.wait("sync", "v_res", r["v_res"])
                    sv = self.store_sync(x_dst[s, t0:t0 + 128, :], xout[:, :], "s_out")
                    xout_free = ("s_out", sv)
                self._xout2_free = xout_free

        self.freebuf(wq2, masks, xlq, qT, ebr, eb, ysub[0], ysub[1], ysub[2],
                     ysub[3], rzb, xres, xout)

    def build(self):
        self._eb_free = {}
        self._pair_e = {}
        self.setup()
        l0a_dst = self.out if DBG_STAGE == "L0A" else self.x_a
        self.l0a(self.x_in, l0a_dst)
        self.barrier()
        if DBG_STAGE != "L0A":
            ffn_dst = self.out if DBG_STAGE == "FFN0" else self.x_b
            self.ffn0(self.x_a, ffn_dst)
            self.barrier()
        if DBG_STAGE not in ("L0A", "FFN0"):
            l1a_dst = self.out if DBG_STAGE == "L1A" else self.x_a
            self.l1a(self.x_b, l1a_dst)
            self.barrier()
        if DBG_STAGE is None:
            self._xout2_free = None
            self._ebr_free = {}
            self.l1f(self.x_a, self.out)
            self.barrier()
        self.wait("sync", "s_out", self.val("s_out"))
        return self.finalize()


def _bcols(b):
    n = b.shape[0]
    if n < 128:
        return np.ascontiguousarray(b.reshape(n, 1))
    return np.ascontiguousarray(b.reshape(n // 128, 128).T)


def _prep_shared(inputs):
    T = lambda k: np.ascontiguousarray(np.asarray(inputs[k]).T)
    C = lambda k: _bcols(np.asarray(inputs[k]))
    A = lambda k: np.ascontiguousarray(np.asarray(inputs[k]))
    return {
        "w_dkvT": T("L0_attn_W_down_kv"), "w_upkT": T("L0_attn_W_up_k"),
        "w_upvT": T("L0_attn_W_up_v"), "w_dqT": T("L0_attn_W_down_q"),
        "w_upqT": T("L0_attn_W_up_q"), "w_qrT": T("L0_attn_W_qr"),
        "w_krT": T("L0_attn_W_kr"), "w_fcT": T("L0_attn_W_fc"),
        "b_dkv_c": C("L0_attn_b_down_kv"), "b_upk_c": C("L0_attn_b_up_k"),
        "b_upv_c": C("L0_attn_b_up_v"), "b_dq_c": C("L0_attn_b_down_q"),
        "b_upq_c": C("L0_attn_b_up_q"), "b_qr_c": C("L0_attn_b_qr"),
        "b_kr_c": C("L0_attn_b_kr"), "b_fc": A("L0_attn_b_fc"),
        "wg0": A("L0_ffn_w_gate"), "wu0": A("L0_ffn_w_up"), "wd0": A("L0_ffn_w_down"),
        "w_qT1": T("L1_attn_q_w"), "w_kT1": T("L1_attn_k_w"),
        "w_vT1": T("L1_attn_v_w"), "w_oT1": T("L1_attn_o_w"),
        "b_q1_c": C("L1_attn_q_b"), "b_k1": A("L1_attn_k_b"),
        "b_v1": A("L1_attn_v_b"), "b_o1": A("L1_attn_o_b"),
        "w_gT2": T("L1_ffn_W_gate"), "w_qkvT2": T("L1_ffn_W_qkv"),
    }


def kernel(**inputs):
    shared = _prep_shared(inputs)
    x = np.ascontiguousarray(np.asarray(inputs["L0_attn_h"], dtype=np.float32))
    mk = MK({k: v.shape for k, v in shared.items()})
    nc = mk.build()
    in_maps = []
    for c in range(NCORES):
        m = dict(shared)
        m["x_in"] = np.ascontiguousarray(x[c * SPC:(c + 1) * SPC])
        in_maps.append(m)
    res = run_bass_kernel_spmd(nc, in_maps, core_ids=list(range(NCORES)))
    return np.concatenate([r["out"] for r in res.results], axis=0)
